# version 16
# baseline (speedup 1.0000x reference)
import numpy as np
import ml_dtypes
from contextlib import ExitStack

import concourse.bass as bass
import concourse.mybir as mybir
from concourse.bass_utils import run_bass_kernel_spmd

F32 = mybir.dt.float32
BF16 = mybir.dt.bfloat16
U32 = mybir.dt.uint32
AF = mybir.ActivationFunctionType
ALU = mybir.AluOpType
AX = mybir.AxisListType

P = 128
D = 1024
DT = 8
SUP = 4
NSUP = 4
NCH = 16
TOWN = 2048
LR = 0.01
EPS = 1e-5
NE = 8
QUEUES = ("tensor", "vector", "scalar", "gpsimd", "sync")
QSEM = {"vector": "vs", "scalar": "ss", "gpsimd": "gs", "sync": "do", "tensor": "ts"}


class Prog:

    def __init__(self, nc, ctx):
        self.nc = nc
        self.ctx = ctx
        self.q = {k: [] for k in QUEUES}
        self.cnt = {}
        self.hand = {}
        self.lastw = {k: {} for k in QUEUES}

    def sem(self, name):
        if name not in self.hand:
            self.hand[name] = self.ctx.enter_context(self.nc.semaphore(name))
            self.cnt[name] = 0
        return self.hand[name]

    def tok(self, name):
        self.sem(name)
        return (name, self.cnt[name])

    def emit(self, queue, fn, waits=(), inc=None, inc_by=1):
        ws = []
        for w in waits:
            if w is None:
                continue
            s, v = w
            if v <= 0:
                continue
            if self.lastw[queue].get(s, 0) >= v:
                continue
            self.lastw[queue][s] = v
            self.sem(s)
            ws.append((s, v))
        tk = None
        it = None
        if inc is not None:
            self.sem(inc)
            self.cnt[inc] += inc_by
            tk = (inc, self.cnt[inc])
            it = (inc, inc_by)
        self.q[queue].append((ws, fn, it))
        return tk

    def replay(self, queue, eng):
        for ws, fn, it in self.q[queue]:
            for s, v in ws:
                eng.wait_ge(self.hand[s], v)
            ins = fn(eng)
            if it is not None:
                assert ins is not None, f"{queue}: inc {it} but fn returned None"
                ins.then_inc(self.hand[it[0]], it[1])


class Bank:

    def __init__(self, pr, name, dtype=F32):
        width = 512 if dtype == F32 else 1024
        self.pr = pr
        self.t = pr.ctx.enter_context(pr.nc.psum_tensor(name, [P, width], dtype))
        self.w = None
        self.readers = []

    def mm_group(self, mms, extra_waits=(), cont=False, open_end=False):
        pr = self.pr
        n = len(mms)
        waits = [w for w in list(extra_waits) + ([] if cont else self.readers) if w is not None]
        tk = None
        for i, (o, l, r) in enumerate(mms):
            tk = pr.emit(
                "tensor",
                (lambda t, o=o, l=l, r=r, st=(i == 0 and not cont),
                 sp=(i == n - 1 and not open_end):
                 t.matmul(o, l, r, start=st, stop=sp)),
                waits=waits if i == 0 else (),
                inc="ts" if i == n - 1 else None)
        self.w = tk
        if not cont:
            self.readers = []
        return tk

    def read(self, queue, fn, extra_waits=()):
        tk = self.pr.emit(queue, fn, waits=[self.w] + list(extra_waits), inc=QSEM[queue])
        self.readers.append(tk)
        return tk


def ap(t, offset, dims):
    return bass.AP(t, offset, dims)


_BUILD_N = [0]


def build_program(debug=False, n_super=2 * NSUP, do_passb=True):
    _BUILD_N[0] += 1
    _tp = "tap_" if _BUILD_N[0] == 1 else f"tap{_BUILD_N[0]}_"
    nc = bass.Bass("TRN2", target_bir_lowering=False)

    d_xm = nc.dram_tensor("xm", [TOWN, D], F32, kind="ExternalInput")
    d_xp = nc.dram_tensor("xp", [TOWN, D], F32, kind="ExternalInput")
    d_gmask = nc.dram_tensor("gmask", [P, 2], F32, kind="ExternalInput")
    d_cvec = nc.dram_tensor("cvec", [D, 1], F32, kind="ExternalInput")
    d_wmod = nc.dram_tensor("wmod", [D, 6 * D], F32, kind="ExternalInput")
    d_thK = nc.dram_tensor("thK", [D, D], F32, kind="ExternalInput")
    d_thV = nc.dram_tensor("thV", [D, D], F32, kind="ExternalInput")
    d_thQ = nc.dram_tensor("thQ", [D, D], F32, kind="ExternalInput")
    d_w0T = nc.dram_tensor("w0T", [D, D], F32, kind="ExternalInput")
    d_b0 = nc.dram_tensor("b0r", [1, D], F32, kind="ExternalInput")
    d_rw = nc.dram_tensor("rw", [D, NE], F32, kind="ExternalInput")
    d_gw = nc.dram_tensor("gw", [NE, D, D], F32, kind="ExternalInput")
    d_uw = nc.dram_tensor("uw", [NE, D, D], F32, kind="ExternalInput")
    d_dw = nc.dram_tensor("dw", [NE, D, D], F32, kind="ExternalInput")
    d_sgw = nc.dram_tensor("sgw", [D, 2 * D], F32, kind="ExternalInput")
    d_suw = nc.dram_tensor("suw", [D, 2 * D], F32, kind="ExternalInput")
    d_sdw = nc.dram_tensor("sdw", [2 * D, D], F32, kind="ExternalInput")
    d_ident = nc.dram_tensor("identb", [P, P], BF16, kind="ExternalInput")
    d_mune = nc.dram_tensor("mune", [P, P], BF16, kind="ExternalInput")
    d_mlne = nc.dram_tensor("mlne", [P, P], BF16, kind="ExternalInput")
    d_muinc = nc.dram_tensor("muinc", [P, P], BF16, kind="ExternalInput")

    d_out = nc.dram_tensor("out", [TOWN, D], F32, kind="ExternalOutput")

    d_x2s = nc.dram_tensor("x2s", [TOWN, D], F32)
    d_h2s = nc.dram_tensor("h2s", [NCH * P, D], BF16)
    d_modscr = nc.dram_tensor("modscr", [1, 6 * D], F32)
    d_thq_scr = nc.dram_tensor("thq_scr", [D, D], BF16)
    d_gw_scr = nc.dram_tensor("gw_scr", [NE, P, DT * D], BF16)
    d_uw_scr = nc.dram_tensor("uw_scr", [NE, P, DT * D], BF16)
    d_dw_scr = nc.dram_tensor("dw_scr", [NE, P, DT * D], BF16)
    d_sg_scr = nc.dram_tensor("sg_scr", [2, P, DT * D], BF16)
    d_su_scr = nc.dram_tensor("su_scr", [2, P, DT * D], BF16)
    d_sd_scr = nc.dram_tensor("sd_scr", [2, P, DT * D], BF16)

    taps = {}

    with ExitStack() as octx:
        pr = Prog(nc, octx)
        E = octx.enter_context

        def tap(name, apv, waits):
            if not debug:
                return
            dt_ = nc.dram_tensor(_tp + name, list(apv.shape), apv.dtype, kind="ExternalOutput")
            taps[name] = True
            pr.emit("sync", lambda sy, a=apv, d2=dt_: sy.dma_start(d2[...], a),
                    waits=waits, inc="do", inc_by=16)

        s_ident = E(nc.sbuf_tensor("s_ident", [P, P], BF16))
        s_mune = E(nc.sbuf_tensor("s_mune", [P, P], BF16))
        s_mlne = E(nc.sbuf_tensor("s_mlne", [P, P], BF16))
        s_muinc = E(nc.sbuf_tensor("s_muinc", [P, P], BF16))
        ones_r1 = E(nc.sbuf_tensor("ones_r1", [1, P], BF16))
        ones_row = E(nc.sbuf_tensor("ones_row", [1, 512], BF16))
        ones_r1f = E(nc.sbuf_tensor("ones_r1f", [1, P], F32))
        ones_cl = E(nc.sbuf_tensor("ones_cl", [P, 1], BF16))
        onef_col = E(nc.sbuf_tensor("onef_col", [P, 1], F32))
        s_gmask = E(nc.sbuf_tensor("s_gmask", [P, 2], F32))
        gb_bc = E(nc.sbuf_tensor("gb_bc", [P, D], F32))
        ga_bc = E(nc.sbuf_tensor("ga_bc", [P, D], F32))
        comb_all = E(nc.sbuf_tensor("comb_all", [P, NCH * NE], F32))
        maskf = E(nc.sbuf_tensor("maskf", [P, NE], F32))
        mk0 = E(nc.sbuf_tensor("mk0", [P, NE], F32))
        tmp8 = E(nc.sbuf_tensor("tmp8", [P, NE], F32))
        b_f32 = E(nc.sbuf_tensor("b_f32", [1, D], F32))
        b_bfA = E(nc.sbuf_tensor("b_bfA", [1, D], BF16))
        b_bfB = E(nc.sbuf_tensor("b_bfB", [1, D], BF16))
        xb = E(nc.sbuf_tensor("xb", [P, 2 * D], F32))
        lnt = E(nc.sbuf_tensor("lnt", [P, 16], F32))
        hbf = E(nc.sbuf_tensor("hbf", [P, D], BF16))
        lg = E(nc.sbuf_tensor("lg", [P, NE], F32))
        zex = E(nc.sbuf_tensor("zex", [P, NE], F32))
        sc8 = E(nc.sbuf_tensor("sc8", [P, NE], F32))
        top8 = E(nc.sbuf_tensor("top8", [P, 8], F32))
        rcol = E(nc.sbuf_tensor("rcol", [P, 4], F32))
        h2stg = E(nc.sbuf_tensor("h2stg", [P, D], BF16))
        ca1c = E(nc.sbuf_tensor("ca1c", [P, DT], F32))
        cb1c = E(nc.sbuf_tensor("cb1c", [P, DT], F32))
        sa_f = E(nc.sbuf_tensor("sa_f", [P, DT], F32))
        sb_f = E(nc.sbuf_tensor("sb_f", [P, DT], F32))
        rw_bf = E(nc.sbuf_tensor("rw_bf", [P, DT * NE], BF16))

        bk = [Bank(pr, f"bk{i}") for i in range(7)]
        bkT = Bank(pr, "bkT", BF16)

        cdb = E(nc.sbuf_tensor("cdb", [P, 4], F32))
        pr.emit("gpsimd", lambda g: g.memset(cdb[:, 0:1], 2 * LR / D), inc="gs")
        pr.emit("gpsimd", lambda g: g.memset(cdb[:, 1:2], 1.0), inc="gs")
        pr.emit("gpsimd", lambda g: g.memset(cdb[:, 2:3], 0.0), inc="gs")
        pr.emit("gpsimd", lambda g: g.memset(cdb[:, 3:4], -1.0), inc="gs")
        nc.const_aps.register(F32, 2 * LR / D, cdb[:, 0:1])
        nc.const_aps.register(F32, 1.0, cdb[:, 1:2])
        nc.const_aps.register(F32, 0.0, cdb[:, 2:3])
        nc.const_aps.register(F32, -1.0, cdb[:, 3:4])

        for t_, d_ in ((s_ident, d_ident), (s_mune, d_mune), (s_mlne, d_mlne), (s_muinc, d_muinc)):
            pr.emit("sync", lambda sy, t2=t_, d2=d_: sy.dma_start(t2[:, :], d2[:, :]),
                    inc="dini", inc_by=16)
        pr.emit("sync", lambda sy: sy.dma_start(s_gmask[:, :], d_gmask[:, :]), inc="dini", inc_by=16)
        pr.emit("sync", lambda sy: sy.dma_start(b_f32[:, :], d_b0[:, :]), inc="dini", inc_by=16)
        pr.emit("gpsimd", lambda g: g.memset(ones_r1[:, :], 1.0), inc="gs")
        pr.emit("gpsimd", lambda g: g.memset(ones_row[:, :], 1.0), inc="gs")
        pr.emit("gpsimd", lambda g: g.memset(ones_cl[:, :], 1.0), inc="gs")
        pr.emit("gpsimd", lambda g: g.memset(ones_r1f[:, :], 1.0), inc="gs")
        pr.emit("gpsimd", lambda g: g.memset(onef_col[:, :], 1.0), inc="gs")
        t_dl0 = pr.tok("dini")
        t_ms0 = pr.tok("gs")

        t_bb0 = pr.emit("scalar", lambda s: s.copy(b_bfA[0:1, :], b_f32[0:1, :]),
                        waits=[t_dl0, t_ms0], inc="ss")

        wprep_toks = {}


        with ExitStack() as actx:
            EA = actx.enter_context

            with ExitStack() as wctx:
                wmb = wctx.enter_context(nc.sbuf_tensor("wmb", [P, 2 * 6 * D], F32))
                silc = wctx.enter_context(nc.sbuf_tensor("silc", [P, DT], F32))
                modr = wctx.enter_context(nc.sbuf_tensor("modr", [1, 6 * D], F32))
                tmpr = wctx.enter_context(nc.sbuf_tensor("tmpr", [1, 3 * D], F32))

                t_cv = pr.emit("sync", lambda sy: sy.dma_start(
                    silc[:, :], ap(d_cvec, 0, [[1, P], [P, DT]]),
                    allow_slow_non_contiguous=True), inc="dcv", inc_by=16)
                t_sil = pr.emit("scalar", lambda s: s.activation(silc[:, :], silc[:, :], AF.Silu),
                                waits=[t_cv], inc="ss")
                prev_mm = [None] * DT
                for pas in range(2):
                    mm_toks = []
                    for dt_i in range(DT):
                        slot = dt_i % 2
                        wl = [mm_toks[dt_i - 2]] if dt_i >= 2 else [prev_mm[DT - 2 + dt_i]]
                        tkd = pr.emit("sync", lambda sy, dt_i=dt_i, slot=slot: sy.dma_start(
                            ap(wmb, slot * 6 * D, [[2 * 6 * D, P], [1, 6 * D]]),
                            ap(d_wmod, dt_i * P * 6 * D, [[6 * D, P], [1, 6 * D]])),
                            waits=wl, inc=f"dwm{slot}", inc_by=16)
                        tk = None
                        for jj in range(6):
                            jb = pas * 6 + jj
                            b2 = bk[jj]
                            o = b2.t[0:1, 0:512]
                            l = silc[:, dt_i:dt_i + 1]
                            r = wmb[:, slot * 6 * D + jb * 512:slot * 6 * D + (jb + 1) * 512]
                            wl2 = [tkd]
                            if dt_i == 0:
                                wl2 += [t_sil] + b2.readers
                            tk = pr.emit("tensor",
                                         (lambda t, o=o, l=l, r=r, st=(dt_i == 0), sp=(dt_i == DT - 1):
                                          t.matmul(o, l, r, start=st, stop=sp)),
                                         waits=wl2, inc="ts" if jj == 5 else None)
                        mm_toks.append(tk)
                        if dt_i == DT - 1:
                            for jj in range(6):
                                bk[jj].w = tk
                                bk[jj].readers = []
                    for jj in range(6):
                        jb = pas * 6 + jj
                        bk[jj].read("vector", lambda v, jj=jj, jb=jb: v.tensor_copy(
                            modr[0:1, jb * 512:(jb + 1) * 512], bk[jj].t[0:1, 0:512]))
                    prev_mm = mm_toks
                t_mod = pr.tok("vs")

                t_gdone = t_mod
                for idx in (2, 5):
                    g_sl = modr[0:1, idx * D:(idx + 1) * D]
                    a_sl = tmpr[0:1, 0:D]
                    bb_sl = tmpr[0:1, D:2 * D]
                    s_sl = tmpr[0:1, 2 * D:3 * D]
                    pr.emit("vector", lambda v, g=g_sl, a=a_sl: v.tensor_scalar_max(a, g, 0.25),
                            waits=[t_mod, t_gdone], inc="vs")
                    pr.emit("vector", lambda v, g=g_sl, b3=bb_sl: v.tensor_scalar_min(b3, g, -0.25), inc="vs")
                    tc0 = pr.emit("vector", lambda v, g=g_sl, s2=s_sl: v.tensor_single_scalar(s2, g, 0.0, op=ALU.is_ge), inc="vs")
                    tc1 = pr.emit("vector", lambda v, a=a_sl, b3=bb_sl: v.tensor_sub(a, a, b3),
                                  waits=[tc0], inc="vs")
                    tc2 = pr.emit("vector", lambda v, a=a_sl, s2=s_sl: v.tensor_mul(a, a, s2),
                                  waits=[tc1], inc="vs")
                    t_gdone = pr.emit("vector", lambda v, g=g_sl, a=a_sl, b3=bb_sl: v.tensor_add(g, b3, a),
                                      waits=[tc2], inc="vs")
                t_clamp = pr.tok("vs")

                for row_i, dst in ((2, ga_bc), (5, gb_bc)):
                    for jh in range(2):
                        b2 = bk[6]
                        b2.mm_group(
                            [(b2.t[:, 0:512], ones_r1f[:, :],
                              modr[0:1, row_i * D + jh * 512:row_i * D + (jh + 1) * 512])],
                            extra_waits=[t_clamp, t_ms0])
                        b2.read("vector", lambda v, dst=dst, jh=jh, b2=b2: v.tensor_copy(
                            dst[:, jh * 512:(jh + 1) * 512], b2.t[:, 0:512]))
                t_ms = pr.emit("sync", lambda sy: sy.dma_start(d_modscr[:, :], modr[:, :]),
                               waits=[t_clamp], inc="dms", inc_by=16)
                t_l1 = pr.emit("sync", lambda sy: sy.dma_start(
                    ca1c[:, :], ap(d_modscr, 1 * D, [[1, P], [P, DT]]),
                    allow_slow_non_contiguous=True), waits=[t_ms], inc="dca", inc_by=16)
                t_l2 = pr.emit("sync", lambda sy: sy.dma_start(
                    cb1c[:, :], ap(d_modscr, 4 * D, [[1, P], [P, DT]]),
                    allow_slow_non_contiguous=True), waits=[t_ms], inc="dcb", inc_by=16)
                t_l3 = pr.emit("gpsimd", lambda g: g.dma_start(
                    sa_f[:, :], ap(d_modscr, 0 * D, [[1, P], [P, DT]]),
                    allow_slow_non_contiguous=True), waits=[t_ms], inc="dsa", inc_by=16)
                t_l4 = pr.emit("gpsimd", lambda g: g.dma_start(
                    sb_f[:, :], ap(d_modscr, 3 * D, [[1, P], [P, DT]]),
                    allow_slow_non_contiguous=True), waits=[t_ms], inc="dsb2", inc_by=16)
                t_c1 = pr.emit("vector", lambda v: v.tensor_scalar_add(ca1c[:, :], ca1c[:, :], 1.0),
                               waits=[t_l1], inc="vs")
                t_c2 = pr.emit("vector", lambda v: v.tensor_scalar_add(cb1c[:, :], cb1c[:, :], 1.0),
                               waits=[t_l2], inc="vs")
                t_wmod_ts = pr.tok("ts")
                t_wmod_vs = pr.tok("vs")

            thK = EA(nc.sbuf_tensor("s_thK", [P, DT * D], BF16))
            thVn = EA(nc.sbuf_tensor("s_thVn", [P, DT * D], BF16))
            thQp = EA(nc.sbuf_tensor("s_thQp", [P, DT * D], BF16))
            WT = EA(nc.sbuf_tensor("WT", [P, DT * D], F32))
            Wbf = EA(nc.sbuf_tensor("Wbf", [P, DT * D], BF16))
            hft = EA(nc.sbuf_tensor("hft", [P, 2 * DT * 512], BF16))
            Kft = EA(nc.sbuf_tensor("Kft", [P, DT * 512], BF16))
            Ktk = EA(nc.sbuf_tensor("Ktk", [P, SUP * D], BF16))
            Qft = EA(nc.sbuf_tensor("Qft", [P, DT * 512], BF16))
            Gsb = EA(nc.sbuf_tensor("Gsb", [P, SUP * D], BF16))
            Scr = EA(nc.sbuf_tensor("Scr", [P, 6 * P], BF16))
            Sqr = EA(nc.sbuf_tensor("Sqr", [P, 10 * P], BF16))
            Usb = EA(nc.sbuf_tensor("Usb", [P, 8 * P], BF16))
            usb_rd = {j: None for j in range(8)}
            Bp = EA(nc.sbuf_tensor("Bp", [P, 2 * SUP * P], BF16))
            Btp = EA(nc.sbuf_tensor("Btp", [P, 2 * SUP * P], BF16))
            Dsb = EA(nc.sbuf_tensor("Dsb", [P, SUP * P], BF16))
            Xsb = EA(nc.sbuf_tensor("Xsb", [P, SUP * P], BF16))
            RHSs = EA(nc.sbuf_tensor("RHSs", [P, D], BF16))
            x2b = EA(nc.sbuf_tensor("x2b", [P, 2 * D], F32))
            xeb = EA(nc.sbuf_tensor("xeb", [P, 2 * D], F32))

            for dt_i in range(DT):
                pr.emit("gpsimd", lambda g, dt_i=dt_i: g.dma_start(
                    ap(thK, dt_i * D, [[DT * D, P], [1, D]]),
                    ap(d_thK, dt_i * P * D, [[D, P], [1, D]])),
                    waits=[t_wmod_ts, t_wmod_vs], inc="dthk", inc_by=16)
            t_thk_all = pr.tok("dthk")
            thk_toks = [t_thk_all] * DT
            for dt_i in range(DT):
                pr.emit("gpsimd", lambda g, dt_i=dt_i: g.dma_start(
                    ap(thVn, dt_i * D, [[DT * D, P], [1, D]]),
                    ap(d_thV, dt_i * P * D, [[D, P], [1, D]])),
                    waits=[t_wmod_ts, t_wmod_vs], inc="dthv", inc_by=16)
            t_thv_all = pr.tok("dthv")
            thv_toks = [t_thv_all] * DT
            for dt_i in range(DT):
                pr.emit("gpsimd", lambda g, dt_i=dt_i: g.dma_start(
                    ap(thQp, dt_i * D, [[DT * D, P], [1, D]]),
                    ap(d_thQ, dt_i * P * D, [[D, P], [1, D]])),
                    waits=[t_wmod_ts, t_wmod_vs], inc="dthq", inc_by=16)
            t_thq_all = pr.tok("dthq")
            thq_toks = [t_thq_all] * DT

            for dt_i in range(DT):
                pr.emit("scalar", lambda s, dt_i=dt_i: s.activation(
                    thVn[:, dt_i * D:(dt_i + 1) * D], thVn[:, dt_i * D:(dt_i + 1) * D],
                    AF.Identity, scale=-1.0),
                    waits=[thv_toks[dt_i]], inc="ss")
            t_thprime = pr.tok("ss")
            t_qprime = t_thq_all

            t_rwl = pr.emit("gpsimd", lambda g: g.dma_start(
                rw_bf[:, :], ap(d_rw, 0, [[NE, P], [P * NE, DT], [1, NE]])),
                waits=[t_wmod_ts, t_wmod_vs], inc="drw", inc_by=16)


            def wprep_family(name, pairs):
                for s_ap, d_ap in pairs:
                    pr.emit("gpsimd", lambda g, s2=s_ap, d2=d_ap: g.dma_start(d2, s2),
                            inc="dp", inc_by=16)
                wprep_toks[name] = pr.tok("dp")

            def gu_pairs(src, dst):
                out = []
                for e2 in range(NE):
                    for ht in range(DT):
                        out.append((ap(src, e2 * D * D + ht * P * D, [[D, P], [1, D]]),
                                    ap(dst, e2 * P * DT * D + ht * D, [[DT * D, P], [1, D]])))
                return out
            wprep_family("gw", gu_pairs(d_gw, d_gw_scr))
            wprep_family("uw", gu_pairs(d_uw, d_uw_scr))
            wprep_family("dw", gu_pairs(d_dw, d_dw_scr))

            def sh_pairs(src, dst):
                out = []
                for fh in range(2):
                    for ht in range(DT):
                        out.append((ap(src, ht * P * 2 * D + fh * D, [[2 * D, P], [1, D]]),
                                    ap(dst, fh * P * DT * D + ht * D, [[DT * D, P], [1, D]])))
                return out
            wprep_family("sgw", sh_pairs(d_sgw, d_sg_scr))
            wprep_family("suw", sh_pairs(d_suw, d_su_scr))
            sd_pairs = []
            for fh in range(2):
                for fj in range(DT):
                    sd_pairs.append((ap(d_sdw, (fh * D + fj * P) * D, [[D, P], [1, D]]),
                                     ap(d_sd_scr, fh * P * DT * D + fj * D, [[DT * D, P], [1, D]])))
            wprep_family("sdw", sd_pairs)
            dp_all = pr.tok("dp")
            for k2 in ("gw", "uw", "dw", "sgw", "suw", "sdw"):
                wprep_toks[k2] = dp_all

            for dt_i in range(DT):
                pr.emit("sync", lambda sy, dt_i=dt_i: sy.dma_start(
                    ap(WT, dt_i * D, [[DT * D, P], [1, D]]),
                    ap(d_w0T, dt_i * P * D, [[D, P], [1, D]])),
                    waits=[t_wmod_ts, t_wmod_vs], inc="dw0t", inc_by=16)
            t_w0_all = pr.tok("dw0t")
            for dt_i in range(DT):
                pr.emit("scalar", lambda s, dt_i=dt_i: s.copy(
                    Wbf[:, dt_i * D:(dt_i + 1) * D], WT[:, dt_i * D:(dt_i + 1) * D]),
                    waits=[t_w0_all], inc="ss")
            t_w0 = pr.tok("ss")

            t_prologue = [pr.tok("ts"), pr.tok("vs"), pr.tok("ss"), pr.tok("gs"),
                          t_w0, t_bb0, t_thprime, t_qprime,
                          t_thk_all, t_thq_all, t_rwl, t_l3, t_l4]

            ln_done = {}
            x_dma = {}
            x2_spill_tok = None
            h2_guard = [None, None]
            last_cp = [None]
            last_hn = [None]
            last_tr = [None]

            def x_src(s_abs, ci):
                d_src = d_xp if s_abs < NSUP else d_xm
                c16 = (s_abs % NSUP) * SUP + ci
                return ap(d_src, c16 * P * D, [[D, P], [1, D]])

            def ensure_xdma(cl):
                if cl >= 2 * NSUP * SUP or cl in x_dma:
                    return
                slot = cl % 2
                wl = list(t_prologue) if cl == 0 else []
                if cl - 2 >= 0:
                    wl += list(ln_done[cl - 2])
                x_dma[cl] = pr.emit("sync", lambda sy, cl=cl, slot=slot: sy.dma_start(
                    ap(xb, slot * D, [[2 * D, P], [1, D]]), x_src(cl // SUP, cl % SUP)),
                    waits=wl, inc=f"dx{slot}", inc_by=16)

            def pair_id_s(c, cp):
                return c * (c - 1) // 2 + cp

            def pair_id_q(c, cp):
                return c * (c + 1) // 2 + cp

            def emit_ln(x_ap, wait0, hazard_guard):
                pr.emit("vector", lambda v, x3=x_ap: v.bn_stats(lnt[:, 0:6], x3[:, 0:512]),
                        waits=wait0 + [last_hn[0]], inc="vs")
                t_b2 = pr.emit("vector", lambda v, x3=x_ap: v.bn_stats(lnt[:, 6:12], x3[:, 512:1024]), inc="vs")
                t_ag = pr.emit("vector", lambda v: v.bn_aggr(lnt[:, 12:14], lnt[:, 0:12]),
                               waits=[t_b2], inc="vs")
                t_v1 = pr.emit("vector", lambda v: v.tensor_scalar_add(lnt[:, 14:15], lnt[:, 13:14], EPS),
                               waits=[t_ag], inc="vs")
                t_s1 = pr.emit("scalar", lambda s: s.activation(lnt[:, 14:15], lnt[:, 14:15], AF.Sqrt),
                               waits=[t_v1], inc="ss")
                t_rc = pr.emit("vector", lambda v: v.reciprocal(lnt[:, 15:16], lnt[:, 14:15]),
                               waits=[t_s1], inc="vs")
                t_v3 = pr.emit("vector", lambda v: v.scalar_tensor_tensor(
                    lnt[:, 14:15], lnt[:, 12:13], -1.0, lnt[:, 15:16], op0=ALU.mult, op1=ALU.mult),
                    waits=[t_rc], inc="vs")
                t_hn = pr.emit("scalar", lambda s, x3=x_ap: s.activation(
                    hbf[:, :], x3, AF.Identity, bias=lnt[:, 14:15], scale=lnt[:, 15:16]),
                    waits=[t_v3] + hazard_guard, inc="ss")
                last_hn[0] = t_hn
                return t_v3, t_hn

            def emit_transposes(t_hn, dst_fns, extra=()):
                tk = None
                for i in range(DT):
                    tk = pr.emit("tensor",
                                 (lambda t, i=i: t.transpose(
                                     bkT.t[:, i * P:(i + 1) * P], hbf[:, i * P:(i + 1) * P],
                                     s_ident[:, :])),
                                 waits=([t_hn] + bkT.readers) if i == 0 else (),
                                 inc="ts" if i == DT - 1 else None)
                bkT.w = tk
                bkT.readers = []
                last_tr[0] = tk
                tk = None
                for fi, fn in enumerate(dst_fns):
                    tk = bkT.read("scalar", fn, extra_waits=list(extra) if fi == 0 else ())
                return tk

            def emit_state_update(t_chain, t_outs, npar_buf):
                for dt_i in range(DT):
                    for jh in range(2):
                        b2 = bk[(dt_i * 2 + jh) % 2]
                        mms = []
                        for c in range(SUP):
                            mms.append((b2.t[:, 0:512],
                                        Ktk[:, c * D + dt_i * P:c * D + (dt_i + 1) * P],
                                        Gsb[:, c * D + jh * 512:c * D + (jh + 1) * 512]))
                        b2.mm_group(mms, extra_waits=[t_chain])
                        b2.read("vector", lambda v, b2=b2, dt_i=dt_i, jh=jh: v.tensor_add(
                            WT[:, dt_i * D + jh * 512:dt_i * D + (jh + 1) * 512],
                            WT[:, dt_i * D + jh * 512:dt_i * D + (jh + 1) * 512],
                            b2.t[:, 0:512]))
                t_wupd = pr.tok("vs")
                for dt_i in range(DT):
                    pr.emit("scalar", lambda s, dt_i=dt_i: s.copy(
                        Wbf[:, dt_i * D:(dt_i + 1) * D], WT[:, dt_i * D:(dt_i + 1) * D]),
                        waits=[t_wupd, t_outs], inc="ss")
                for jh in range(2):
                    b2 = bk[2 + jh]
                    mms = []
                    for c in range(SUP):
                        mms.append((b2.t[0:1, 0:512], ones_cl[:, :],
                                    Gsb[:, c * D + jh * 512:c * D + (jh + 1) * 512]))
                    b2.mm_group(mms, extra_waits=[t_chain])
                    b2.read("vector", lambda v, b2=b2, jh=jh: v.tensor_add(
                        b_f32[0:1, jh * 512:(jh + 1) * 512],
                        b_f32[0:1, jh * 512:(jh + 1) * 512], b2.t[0:1, 0:512]))
                t_bupd = pr.tok("vs")
                pr.emit("scalar", lambda s, nb=npar_buf: s.copy(nb[0:1, :], b_f32[0:1, :]),
                        waits=[t_bupd, t_outs], inc="ss")

            pending_state = None
            t_hft_map = {}

            def emit_lnblock(s2, guard):
                hoff = (s2 % 2) * DT * 512
                for ci in range(SUP):
                    cl = s2 * SUP + ci
                    ensure_xdma(cl)
                    ensure_xdma(cl + 1)
                    slot = cl % 2
                    xsl = xb[:, slot * D:(slot + 1) * D]
                    t_v3, t_hn = emit_ln(xsl, [x_dma[cl]], [last_tr[0], last_cp[0]])
                    ln_done[cl] = (t_v3, t_hn)
                    ensure_xdma(cl + 2)
                    emit_transposes(t_hn, [
                        (lambda s, ci=ci, dt=dt, hoff=hoff: s.activation(
                            hft[:, hoff + dt * 512 + ci * P:hoff + dt * 512 + ci * P + P],
                            bkT.t[:, dt * P:(dt + 1) * P], AF.Identity,
                            scale=ca1c[:, dt:dt + 1], bias=sa_f[:, dt:dt + 1]))
                        for dt in range(DT)],
                        extra=[guard])
                t_hft_map[s2] = pr.tok("ss")

            for s_abs in range(n_super):
                is_m = s_abs >= NSUP
                hoff = (s_abs % 2) * DT * 512
                par_buf = b_bfA if s_abs % 2 == 0 else b_bfB
                npar_buf = b_bfB if s_abs % 2 == 0 else b_bfA
                gcol = 1 if is_m else 0
                t_guard_a = pr.tok("ts")
                if pending_state is not None:
                    emit_state_update(*pending_state)
                    pending_state = None
                t_super_ts0 = pr.tok("ts")

                if s_abs == 0:
                    emit_lnblock(0, t_guard_a)
                t_hft = t_hft_map[s_abs]

                for et in range(DT):
                    b2 = bk[2 + et % 2]
                    mms = []
                    for dt_i in range(DT):
                        mms.append((b2.t[:, 0:512],
                                    thK[:, dt_i * D + et * P:dt_i * D + (et + 1) * P],
                                    hft[:, hoff + dt_i * 512:hoff + dt_i * 512 + 512]))
                    b2.mm_group(mms, extra_waits=[t_hft])
                    b2.read("scalar", lambda s, et=et, b2=b2: s.copy(
                        Kft[:, et * 512:(et + 1) * 512], b2.t[:, 0:512]),
                        extra_waits=[t_super_ts0])
                if is_m:
                    for et in range(DT):
                        b2 = bk[4 + et % 2]
                        mms = []
                        for dt_i in range(DT):
                            mms.append((b2.t[:, 0:512],
                                        thQp[:, dt_i * D + et * P:dt_i * D + (et + 1) * P],
                                        hft[:, hoff + dt_i * 512:hoff + dt_i * 512 + 512]))
                        b2.mm_group(mms, extra_waits=[t_hft])
                        b2.read("scalar", lambda s, et=et, b2=b2: s.copy(
                            Qft[:, et * 512:(et + 1) * 512], b2.t[:, 0:512]),
                            extra_waits=[t_super_ts0])
                t_proj = [pr.tok("ss"), pr.tok("vs")]

                for cp in range(SUP):
                    b2 = bk[cp % 2]
                    mms = []
                    for et in range(DT):
                        mms.append((b2.t[:, 0:512],
                                    Kft[:, et * 512 + cp * P:et * 512 + (cp + 1) * P],
                                    Kft[:, et * 512:et * 512 + 512]))
                    b2.mm_group(mms, extra_waits=t_proj)
                    usl = Usb[:, cp * P:(cp + 1) * P]
                    t_u = b2.read("scalar", lambda s, b2=b2, cp=cp, usl=usl: s.activation(
                        usl, b2.t[:, cp * P:(cp + 1) * P], AF.Identity,
                        scale=2 * LR / D, bias=2 * LR / D),
                        extra_waits=[usb_rd[cp]])
                    pr.emit("vector", lambda v, cp=cp, usl=usl: v.tensor_mul(
                        Bp[:, cp * P:(cp + 1) * P], usl, s_mune[:, :]),
                        waits=[t_u, t_super_ts0], inc="vs")
                    usb_rd[cp] = pr.emit("vector", lambda v, cp=cp, usl=usl: v.tensor_mul(
                        Btp[:, cp * P:(cp + 1) * P], usl, s_mlne[:, :]),
                        waits=[t_u], inc="vs")
                    for c in range(cp + 1, SUP):
                        pid = pair_id_s(c, cp)
                        b2.read("scalar", lambda s, b2=b2, pid=pid, c=c: s.activation(
                            Scr[:, pid * P:(pid + 1) * P], b2.t[:, c * P:(c + 1) * P],
                            AF.Identity, scale=1.0, bias=1.0),
                            extra_waits=[t_super_ts0])
                if is_m:
                    for cp in range(SUP):
                        b2 = bk[2 + cp % 2]
                        mms = []
                        for et in range(DT):
                            mms.append((b2.t[:, 0:512],
                                        Kft[:, et * 512 + cp * P:et * 512 + (cp + 1) * P],
                                        Qft[:, et * 512:et * 512 + 512]))
                        b2.mm_group(mms, extra_waits=t_proj)
                        pid = pair_id_q(cp, cp)
                        usl = Usb[:, (4 + cp) * P:(5 + cp) * P]
                        t_u = b2.read("scalar", lambda s, b2=b2, cp=cp, usl=usl: s.activation(
                            usl, b2.t[:, cp * P:(cp + 1) * P], AF.Identity, scale=1.0, bias=1.0),
                            extra_waits=[usb_rd[4 + cp]])
                        usb_rd[4 + cp] = pr.emit("vector", lambda v, pid=pid, usl=usl: v.tensor_mul(
                            Sqr[:, pid * P:(pid + 1) * P], usl, s_muinc[:, :]),
                            waits=[t_u, t_super_ts0], inc="vs")
                        for c in range(cp + 1, SUP):
                            pid = pair_id_q(c, cp)
                            b2.read("scalar", lambda s, b2=b2, pid=pid, c=c: s.activation(
                                Sqr[:, pid * P:(pid + 1) * P], b2.t[:, c * P:(c + 1) * P],
                                AF.Identity, scale=1.0, bias=1.0),
                                extra_waits=[t_super_ts0])
                t_pairs = [pr.tok("ss"), pr.tok("vs")]

                for c in range(SUP):
                    tk = None
                    for et in range(DT):
                        tk = pr.emit("tensor",
                                     (lambda t, c=c, et=et: t.transpose(
                                         bkT.t[:, et * P:(et + 1) * P],
                                         Kft[:, et * 512 + c * P:et * 512 + c * P + P],
                                         s_ident[:, :])),
                                     waits=(t_proj + bkT.readers) if et == 0 else (),
                                     inc="ts" if et == DT - 1 else None)
                    bkT.w = tk
                    bkT.readers = []
                    bkT.read("scalar", lambda s, c=c: s.copy(
                        Ktk[:, c * D:(c + 1) * D], bkT.t[:, :]))

                if s_abs + 1 < n_super:
                    emit_lnblock(s_abs + 1, t_super_ts0)

                x_tok = {}
                for c in range(SUP):
                    x_tok[c] = pr.emit("vector", lambda v, c=c: v.tensor_add(
                        Xsb[:, c * P:(c + 1) * P], s_ident[:, :], Bp[:, c * P:(c + 1) * P]),
                        waits=t_pairs + [t_super_ts0], inc="vs")
                ds_free = {c: None for c in range(SUP)}
                for m in range(1, 7):
                    pslot = (m - 1) % 2
                    cslot = m % 2
                    last = (m == 6)
                    bt_tok = {}
                    if not last:
                        for c in range(SUP):
                            bsl = Bp[:, (pslot * SUP + c) * P:(pslot * SUP + c + 1) * P]
                            btsl = Btp[:, (pslot * SUP + c) * P:(pslot * SUP + c + 1) * P]
                            b2 = bk[c % 2]
                            b2.mm_group([(b2.t[:, 0:P], btsl, bsl)],
                                        extra_waits=[x_tok[c]] if m == 1 else [])
                            b2.read("scalar", lambda s, b2=b2, c=c, cslot=cslot: s.copy(
                                Bp[:, (cslot * SUP + c) * P:(cslot * SUP + c + 1) * P], b2.t[:, 0:P]))
                    for c in range(SUP):
                        bsl = Bp[:, (pslot * SUP + c) * P:(pslot * SUP + c + 1) * P]
                        btsl = Btp[:, (pslot * SUP + c) * P:(pslot * SUP + c + 1) * P]
                        b3 = bk[2 + c % 2]
                        b3.mm_group([(b3.t[:, 0:P], bsl, btsl)],
                                    extra_waits=[x_tok[c]] if m == 1 else [])
                        bt_tok[c] = b3.read("scalar", lambda s, b3=b3, c=c, cslot=cslot: s.copy(
                            Btp[:, (cslot * SUP + c) * P:(cslot * SUP + c + 1) * P], b3.t[:, 0:P]))
                    for c in range(SUP):
                        b4 = bk[4 + c % 2]
                        b4.mm_group([(b4.t[:, 0:P],
                                      Btp[:, (cslot * SUP + c) * P:(cslot * SUP + c + 1) * P],
                                      Xsb[:, c * P:(c + 1) * P])],
                                    extra_waits=[bt_tok[c], x_tok[c]])
                        t_d = b4.read("scalar", lambda s, b4=b4, c=c: s.copy(
                            Dsb[:, c * P:(c + 1) * P], b4.t[:, 0:P]),
                            extra_waits=[ds_free[c]])
                        x_tok[c] = pr.emit("vector", lambda v, c=c: v.tensor_add(
                            Xsb[:, c * P:(c + 1) * P], Xsb[:, c * P:(c + 1) * P],
                            Dsb[:, c * P:(c + 1) * P]),
                            waits=[t_d], inc="vs")
                        ds_free[c] = x_tok[c]
                t_newton = pr.tok("vs")

                g_tok = {}
                ep_pend = None

                def emit_epilogue(c, t_xe, t_oap=()):
                    nonlocal x2_spill_tok, h2_guard
                    c16 = (s_abs % NSUP) * SUP + c
                    sl = (c % 2) * D
                    x2sl = x2b[:, sl:sl + D]
                    t_x2m = pr.emit("vector", lambda v, sl=sl: v.tensor_add(
                        x2b[:, sl:sl + D], x2b[:, sl:sl + D], xeb[:, sl:sl + D]),
                        waits=[t_xe] + list(t_oap), inc="vs")
                    x2_spill_tok = pr.emit("sync", lambda sy, c16=c16, x2sl=x2sl: sy.dma_start(
                        ap(d_x2s, c16 * P * D, [[D, P], [1, D]]), x2sl),
                        waits=[t_x2m], inc="dsp", inc_by=16)
                    t_v3, t_hn = emit_ln(x2sl, [t_x2m], [last_tr[0], last_cp[0]])
                    t_h2 = emit_transposes(t_hn, [
                        (lambda s, dt=dt: s.activation(
                            h2stg[:, dt * P:(dt + 1) * P],
                            bkT.t[:, dt * P:(dt + 1) * P], AF.Identity,
                            scale=cb1c[:, dt:dt + 1], bias=sb_f[:, dt:dt + 1]))
                        for dt in range(DT)],
                        extra=[h2_guard[0], h2_guard[1]])
                    h2_spill = pr.emit("sync", lambda sy, c16=c16: sy.dma_start(
                        ap(d_h2s, c16 * P * D, [[D, P], [1, D]]), h2stg[:, :]),
                        waits=[t_h2], inc="dh2s", inc_by=16)
                    b6 = bk[4]
                    mms = []
                    for dt_i in range(DT):
                        mms.append((b6.t[:, 0:NE],
                                    h2stg[:, dt_i * P:(dt_i + 1) * P],
                                    rw_bf[:, dt_i * NE:(dt_i + 1) * NE]))
                    b6.mm_group(mms, extra_waits=[t_h2])
                    h2_guard = [h2_spill, pr.tok("ts")]
                    t_lg = b6.read("vector", lambda v, b6=b6: v.tensor_copy(lg[:, :], b6.t[:, 0:NE]))
                    t_m1 = pr.emit("vector", lambda v: v.reduce_max(rcol[:, 0:1], lg[:, :], axis=AX.X),
                                   waits=[t_lg], inc="vs")
                    t_m2 = pr.emit("vector", lambda v: v.tensor_scalar_mul(rcol[:, 1:2], rcol[:, 0:1], -1.0),
                                   waits=[t_m1], inc="vs")
                    t_e1 = pr.emit("scalar", lambda s: s.activation(
                        zex[:, :], lg[:, :], AF.Exp, bias=rcol[:, 1:2]),
                        waits=[t_m2], inc="ss")
                    t_r1 = pr.emit("vector", lambda v: v.reduce_sum(rcol[:, 2:3], zex[:, :], axis=AX.X),
                                   waits=[t_e1], inc="vs")
                    t_r2 = pr.emit("vector", lambda v: v.reciprocal(rcol[:, 3:4], rcol[:, 2:3]),
                                   waits=[t_r1], inc="vs")
                    t_sc = pr.emit("scalar", lambda s: s.activation(
                        sc8[:, :], zex[:, :], AF.Identity, scale=rcol[:, 3:4]),
                        waits=[t_r2], inc="ss")
                    t_t1 = pr.emit("vector", lambda v: v.reduce_max(
                        top8[:, 0:1], sc8[:, :], axis=AX.X), waits=[t_sc], inc="vs")
                    t_m0 = pr.emit("vector", lambda v: v.scalar_tensor_tensor(
                        mk0[:, :], sc8[:, :], top8[:, 0:1], sc8[:, :],
                        op0=ALU.is_ge, op1=ALU.bypass), waits=[t_t1], inc="vs")
                    t_pm = pr.emit("vector", lambda v: v.tensor_mul(
                        maskf[:, :], mk0[:, :], sc8[:, :]), waits=[t_m0], inc="vs")
                    t_sb2 = pr.emit("vector", lambda v: v.tensor_sub(
                        maskf[:, :], sc8[:, :], maskf[:, :]), waits=[t_pm], inc="vs")
                    t_t2 = pr.emit("vector", lambda v: v.reduce_max(
                        top8[:, 1:2], maskf[:, :], axis=AX.X), waits=[t_sb2], inc="vs")
                    t_mx = pr.emit("vector", lambda v: v.scalar_tensor_tensor(
                        tmp8[:, :], maskf[:, :], top8[:, 1:2], maskf[:, :],
                        op0=ALU.is_ge, op1=ALU.bypass), waits=[t_t2], inc="vs")
                    t_msum = pr.emit("vector", lambda v: v.tensor_add(
                        maskf[:, :], mk0[:, :], tmp8[:, :]), waits=[t_mx], inc="vs")
                    pr.emit("vector", lambda v, c16=c16: v.tensor_mul(
                        comb_all[:, c16 * NE:(c16 + 1) * NE], maskf[:, :], sc8[:, :]),
                        waits=[t_msum], inc="vs")

                for c in range(SUP):
                    bl, bh = (bk[0], bk[1]) if c % 2 == 0 else (bk[2], bk[3])
                    if is_m:
                        t_xe = pr.emit("sync", lambda sy, s_abs=s_abs, c=c: sy.dma_start(
                            xeb[:, (c % 2) * D:(c % 2 + 1) * D], x_src(s_abs, c)),
                            waits=[pr.tok("vs")], inc=f"dxe{c % 2}", inc_by=16)
                    for jh, b2 in ((0, bl), (1, bh)):
                        mms = []
                        for et in range(DT):
                            mms.append((b2.t[:, 0:512],
                                        Kft[:, et * 512 + c * P:et * 512 + (c + 1) * P],
                                        Wbf[:, et * D + jh * 512:et * D + (jh + 1) * 512]))
                        for dt_i in range(DT):
                            mms.append((b2.t[:, 0:512],
                                        hft[:, hoff + dt_i * 512 + c * P:hoff + dt_i * 512 + (c + 1) * P],
                                        thVn[:, dt_i * D + jh * 512:dt_i * D + (jh + 1) * 512]))
                        mms.append((b2.t[:, 0:512], ones_r1[:, :],
                                    par_buf[0:1, jh * 512:(jh + 1) * 512]))
                        b2.mm_group(mms, extra_waits=[t_newton], open_end=(c > 0))
                    if is_m:
                        for jh, b2 in ((0, bk[5]), (1, bk[6])):
                            mms = []
                            for et in range(DT):
                                mms.append((b2.t[:, 0:512],
                                            Qft[:, et * 512 + c * P:et * 512 + (c + 1) * P],
                                            Wbf[:, et * D + jh * 512:et * D + (jh + 1) * 512]))
                            mms.append((b2.t[:, 0:512], ones_r1[:, :],
                                        par_buf[0:1, jh * 512:(jh + 1) * 512]))
                            b2.mm_group(mms, extra_waits=[t_newton], open_end=True)
                    for jh, b2 in ((0, bl), (1, bh)):
                        mms = []
                        for cp in range(c):
                            pid = pair_id_s(c, cp)
                            mms.append((b2.t[:, 0:512], Scr[:, pid * P:(pid + 1) * P],
                                        Gsb[:, cp * D + jh * 512:cp * D + (jh + 1) * 512]))
                        if mms:
                            b2.mm_group(mms, extra_waits=[g_tok[c - 1]], cont=True)
                        b2.read("scalar", lambda s, b2=b2, jh=jh: s.copy(
                            RHSs[0:P, jh * 512:(jh + 1) * 512], b2.t[:, 0:512]))
                    t_rhs = pr.tok("ss")
                    if is_m and ep_pend is not None:
                        emit_epilogue(*ep_pend)
                        ep_pend = None
                    bg = bk[4]
                    for jh in range(2):
                        bg.mm_group([(bg.t[:, 0:512], Xsb[:, c * P:(c + 1) * P],
                                      RHSs[:, jh * 512:(jh + 1) * 512])],
                                    extra_waits=[t_rhs])
                        g_tok[c] = bg.read("scalar", lambda s, b5=bg, c=c, jh=jh, gcol=gcol: s.activation(
                            Gsb[:, c * D + jh * 512:c * D + (jh + 1) * 512], b5.t[:, 0:512],
                            AF.Identity, scale=s_gmask[:, gcol:gcol + 1]),
                            extra_waits=[t_super_ts0])
                    if is_m:
                        t_oap = []
                        for jh, b2 in ((0, bk[5]), (1, bk[6])):
                            mms = []
                            for cp in range(c + 1):
                                pid = pair_id_q(c, cp)
                                mms.append((b2.t[:, 0:512], Sqr[:, pid * P:(pid + 1) * P],
                                            Gsb[:, cp * D + jh * 512:cp * D + (jh + 1) * 512]))
                            b2.mm_group(mms, extra_waits=[g_tok[c]], cont=True)
                            t_oap.append(b2.read("vector", lambda v, b2=b2, jh=jh, c=c: v.tensor_mul(
                                x2b[:, (c % 2) * D + jh * 512:(c % 2) * D + (jh + 1) * 512],
                                b2.t[:, 0:512],
                                ga_bc[:, jh * 512:(jh + 1) * 512]),
                                extra_waits=[pr.tok("ss"), x2_spill_tok]))
                        ep_pend = (c, t_xe, t_oap)
                if is_m and ep_pend is not None:
                    emit_epilogue(*ep_pend)
                    ep_pend = None
                t_chain = pr.tok("ss")

                t_outs = pr.tok("ts")
                pending_state = (t_chain, t_outs, npar_buf)

            if pending_state is not None:
                emit_state_update(*pending_state)
                pending_state = None

            t_phaseA = [pr.tok("ts"), pr.tok("vs"), pr.tok("ss"), pr.tok("dsp"), pr.tok("dh2s"),
                        pr.tok("dx0"), pr.tok("dx1"), pr.tok("dxe0"), pr.tok("dxe1")]

            if debug:
                tap("WT", WT[:, :], t_phaseA)
                tap("Gsb", Gsb[:, :], t_phaseA)
                tap("bf32", b_f32[:, :], t_phaseA)
                tap("comb", comb_all[:, :], t_phaseA)
                tap("Kft", Kft[:, :], t_phaseA)
                tap("Ktk", Ktk[:, :], t_phaseA)
                tap("Qft", Qft[:, :], t_phaseA)
                tap("Xsb", Xsb[:, :], t_phaseA)
                tap("hft", hft[:, :], t_phaseA)
                tap("modcols_ca1", ca1c[:, :], t_phaseA)
                tap("h2s0", h2stg[:, :], t_phaseA)
                tap("ga_bc", ga_bc[:, 0:D], t_phaseA)
            if debug:
                t_phaseA = t_phaseA + [pr.tok("do")]

        if do_passb:
            with ExitStack() as bctx:
                EB = bctx.enter_context
                wG = EB(nc.sbuf_tensor("wG", [P, 2 * DT * D], BF16))
                wU = EB(nc.sbuf_tensor("wU", [P, 2 * DT * D], BF16))
                wD = EB(nc.sbuf_tensor("wD", [P, DT * D], BF16))
                sg2 = EB(nc.sbuf_tensor("sg2", [P, 512], BF16))
                ub2 = EB(nc.sbuf_tensor("ub2", [P, 512], BF16))
                obuf = EB(nc.sbuf_tensor("obuf", [P, D], F32))

                def pe_srcs(kind, idx):
                    if kind == "sh":
                        return (ap(d_sg_scr, idx * P * DT * D, [[DT * D, P], [1, DT * D]]),
                                ap(d_su_scr, idx * P * DT * D, [[DT * D, P], [1, DT * D]]),
                                ap(d_sd_scr, idx * P * DT * D, [[DT * D, P], [1, DT * D]]),
                                [wprep_toks["sgw"], wprep_toks["suw"], wprep_toks["sdw"]])
                    return (ap(d_gw_scr, idx * P * DT * D, [[DT * D, P], [1, DT * D]]),
                            ap(d_uw_scr, idx * P * DT * D, [[DT * D, P], [1, DT * D]]),
                            ap(d_dw_scr, idx * P * DT * D, [[DT * D, P], [1, DT * D]]),
                            [wprep_toks["gw"], wprep_toks["uw"], wprep_toks["dw"]])

                wslot_free = {0: list(t_phaseA), 1: list(t_phaseA)}
                wd_free = list(t_phaseA)
                out_toks = []

                acc = EB(nc.sbuf_tensor("acc", [P, (NCH // 2) * D], F32))
                h2b = EB(nc.sbuf_tensor("h2b", [P, 2 * DT * D], BF16))
                act = EB(nc.sbuf_tensor("act", [P, DT * D], BF16))
                pes = [("sh", 0), ("sh", 1)] + [("ex", e2) for e2 in range(NE)]
                t_gumul = [None]
                for th in range(2):
                    t_ac = pr.emit("gpsimd", lambda g: g.memset(acc[:, :], 0.0),
                                   waits=t_phaseA + [pr.tok("vs")], inc="gs")
                    acc_guard = [t_ac]
                    thoff = th * DT * D
                    for ht in range(DT):
                        pr.emit("sync", lambda sy, ht=ht, th=th, thoff=thoff: sy.dma_start(
                            ap(h2b, thoff + ht * D, [[2 * DT * D, P], [P, DT], [1, P]]),
                            ap(d_h2s, th * DT * P * D + P * ht + 0, [[D, P], [P * D, DT], [1, P]])),
                            waits=t_phaseA, inc="dh2", inc_by=16)
                    h2_load = [pr.tok("dh2")]
                    for pi, (kind, idx) in enumerate(pes):
                        slot = pi % 2
                        gsrc, usrc, dsrc, wtoks = pe_srcs(kind, idx)
                        t_lg_ = pr.emit("sync", lambda sy, gsrc=gsrc, slot=slot: sy.dma_start(
                            ap(wG, slot * DT * D, [[2 * DT * D, P], [1, DT * D]]), gsrc),
                            waits=wtoks + wslot_free[slot], inc=f"dwg{slot}", inc_by=16)
                        t_lu_ = pr.emit("sync", lambda sy, usrc=usrc, slot=slot: sy.dma_start(
                            ap(wU, slot * DT * D, [[2 * DT * D, P], [1, DT * D]]), usrc),
                            waits=wtoks + wslot_free[slot], inc=f"dwu{slot}", inc_by=16)
                        t_ld_ = pr.emit("sync", lambda sy, dsrc=dsrc: sy.dma_start(wD[:, :], dsrc),
                                        waits=wtoks + wd_free, inc="dwd", inc_by=16)
                        for fj in range(DT):
                            for tb in range(2):
                                bA = bk[(fj * 2 + tb) % 2]
                                bB = bk[4 + (fj * 2 + tb) % 2]
                                mmsA, mmsB = [], []
                                for ht in range(DT):
                                    rh = h2b[:, thoff + ht * D + tb * 512:thoff + ht * D + (tb + 1) * 512]
                                    mmsA.append((bA.t[:, 0:512],
                                                 wG[:, slot * DT * D + ht * D + fj * P:
                                                    slot * DT * D + ht * D + (fj + 1) * P], rh))
                                    mmsB.append((bB.t[:, 0:512],
                                                 wU[:, slot * DT * D + ht * D + fj * P:
                                                    slot * DT * D + ht * D + (fj + 1) * P], rh))
                                bA.mm_group(mmsA, extra_waits=[t_lg_] + h2_load)
                                bB.mm_group(mmsB, extra_waits=[t_lu_] + h2_load)
                                t_sg = bA.read("scalar", lambda s, bA=bA: s.activation(
                                    sg2[:, :], bA.t[:, 0:512], AF.Silu),
                                    extra_waits=[t_gumul[0]])
                                t_ub = bB.read("scalar", lambda s, bB=bB: s.activation(
                                    ub2[:, :], bB.t[:, 0:512], AF.Identity),
                                    extra_waits=[t_gumul[0]])
                                t_gumul[0] = pr.emit("vector", lambda v, fj=fj, tb=tb: v.tensor_mul(
                                    act[:, fj * D + tb * 512:fj * D + (tb + 1) * 512],
                                    sg2[:, :], ub2[:, :]),
                                    waits=[t_sg, t_ub], inc="vs")
                        t_act = pr.tok("vs")
                        for tt in range(DT):
                            c16 = th * 8 + tt
                            for hh in range(2):
                                bC = bk[2 + (tt * 2 + hh) % 2]
                                mms = []
                                for fj in range(DT):
                                    mms.append((bC.t[:, 0:512],
                                                ap(act, fj * D + tt * P, [[DT * D, P], [1, P]]),
                                                wD[:, fj * D + hh * 512:fj * D + (hh + 1) * 512]))
                                bC.mm_group(mms, extra_waits=[t_act, t_ld_])
                                if kind == "sh":
                                    csl = onef_col[:, 0:1]
                                else:
                                    csl = comb_all[:, c16 * NE + idx:c16 * NE + idx + 1]
                                bC.read("vector", lambda v, bC=bC, tt=tt, hh=hh, csl=csl: v.scalar_tensor_tensor(
                                    acc[:, tt * D + hh * 512:tt * D + (hh + 1) * 512],
                                    bC.t[:, 0:512], csl,
                                    acc[:, tt * D + hh * 512:tt * D + (hh + 1) * 512],
                                    op0=ALU.mult, op1=ALU.add),
                                    extra_waits=acc_guard)
                        wslot_free[slot] = [pr.tok("ts")]
                        wd_free = [pr.tok("ts")]
                    t_accdone = pr.tok("vs")
                    for cc in range(8):
                        c16 = th * 8 + cc
                        slot = cc % 2
                        t_xr = pr.emit("gpsimd", lambda g, c16=c16, slot=slot: g.dma_start(
                            ap(xb, slot * D, [[2 * D, P], [1, D]]),
                            ap(d_x2s, c16 * P * D, [[D, P], [1, D]])),
                            waits=[t_accdone] + ([pr.tok("vs")] if cc >= 2 else []),
                            inc=f"dx{slot}", inc_by=16)
                        t_ob = pr.emit("vector", lambda v, cc=cc: v.tensor_mul(
                            obuf[:, :], acc[:, cc * D:(cc + 1) * D], gb_bc[:, :]),
                            waits=[t_accdone, pr.tok("do")], inc="vs")
                        t_o2 = pr.emit("vector", lambda v, slot=slot: v.tensor_add(
                            obuf[:, :], obuf[:, :], xb[:, slot * D:(slot + 1) * D]),
                            waits=[t_xr, t_ob], inc="vs")
                        out_toks.append(pr.emit("gpsimd", lambda g, c16=c16: g.dma_start(
                            ap(d_out, c16 * P * D, [[D, P], [1, D]]), obuf[:, :]),
                            waits=[t_o2], inc="do", inc_by=16))

                pr.q["sync"].append(([out_toks[-1]], lambda sy: None, None))
                pr.lastw["sync"][out_toks[-1][0]] = out_toks[-1][1]

        else:
            pr.q["sync"].append(([t_phaseA[0]], lambda sy: None, None))

        block = octx.enter_context(nc.Block())

        @block.tensor
        def _(t):
            pr.replay("tensor", t)

        @block.vector
        def _(v):
            pr.replay("vector", v)

        @block.scalar
        def _(s):
            pr.replay("scalar", s)

        @block.gpsimd
        def _(g):
            pr.replay("gpsimd", g)

        @block.sync
        def _(sy):
            pr.replay("sync", sy)

    return nc, taps


def _mask(fn):
    r = np.arange(P)[:, None]
    c = np.arange(P)[None, :]
    return fn(r, c).astype(np.float32).astype(ml_dtypes.bfloat16)


def make_inputs(L0_attn_in_seq, L0_attn_theta_K, L0_attn_theta_V, L0_attn_theta_Q,
                L0_attn_W0, L0_attn_b0, L0_ffn_router_w, L0_ffn_gate_w, L0_ffn_up_w,
                L0_ffn_down_w, L0_ffn_shared_gate_w, L0_ffn_shared_up_w,
                L0_ffn_shared_down_w, L0_c, L0_wmod):
    x = np.ascontiguousarray(np.asarray(L0_attn_in_seq, dtype=np.float32))
    f32 = lambda a: np.ascontiguousarray(np.asarray(a, dtype=np.float32))
    base = {
        "thK": f32(np.asarray(L0_attn_theta_K).T),
        "thV": f32(np.asarray(L0_attn_theta_V).T),
        "thQ": f32(np.asarray(L0_attn_theta_Q).T),
        "w0T": f32(np.asarray(L0_attn_W0).T),
        "b0r": f32(np.asarray(L0_attn_b0).reshape(1, D)),
        "rw": f32(L0_ffn_router_w),
        "gw": f32(L0_ffn_gate_w), "uw": f32(L0_ffn_up_w), "dw": f32(L0_ffn_down_w),
        "sgw": f32(L0_ffn_shared_gate_w), "suw": f32(L0_ffn_shared_up_w),
        "sdw": f32(L0_ffn_shared_down_w),
        "wmod": f32(L0_wmod),
        "identb": np.eye(P, dtype=np.float32).astype(ml_dtypes.bfloat16),
        "mune": _mask(lambda r, c: np.where(r < c, -1.0, 0.0)),
        "mlne": _mask(lambda r, c: np.where(r > c, -1.0, 0.0)),
        "muinc": _mask(lambda r, c: np.where(r <= c, 1.0, 0.0)),
    }
    cmat = f32(L0_c)
    in_maps = []
    for k in range(8):
        b, h = k >> 1, k & 1
        gm = np.zeros((P, 2), dtype=np.float32)
        gm[:, 0] = -2.0 * LR / D * h
        gm[:, 1] = -2.0 * LR / D
        m = dict(base)
        m["xm"] = np.ascontiguousarray(x[b, h * TOWN:(h + 1) * TOWN])
        m["xp"] = np.ascontiguousarray(x[b, 0:TOWN])
        m["gmask"] = gm
        m["cvec"] = np.ascontiguousarray(cmat[b].reshape(D, 1))
        in_maps.append(m)
    return in_maps


def kernel(**inputs):
    in_maps = make_inputs(**inputs)
    nc, _ = build_program(debug=False)
    res = run_bass_kernel_spmd(nc, in_maps, core_ids=list(range(8)))
    out = np.empty((4, 4096, D), dtype=np.float32)
    for k in range(8):
        b, h = k >> 1, k & 1
        out[b, h * TOWN:(h + 1) * TOWN] = res.results[k]["out"]
    return out


# version 17 (final)
# speedup vs baseline: 1.0616x; 1.0616x over previous
import numpy as np
import ml_dtypes
from contextlib import ExitStack

import concourse.bass as bass
import concourse.mybir as mybir
from concourse.bass_utils import run_bass_kernel_spmd

F32 = mybir.dt.float32
BF16 = mybir.dt.bfloat16
U32 = mybir.dt.uint32
AF = mybir.ActivationFunctionType
ALU = mybir.AluOpType
AX = mybir.AxisListType

P = 128
D = 1024
DT = 8
SUP = 4
NSUP = 4
NCH = 16
TOWN = 2048
LR = 0.01
EPS = 1e-5
NE = 8
QUEUES = ("tensor", "vector", "scalar", "gpsimd", "sync")
QSEM = {"vector": "vs", "scalar": "ss", "gpsimd": "gs", "sync": "do", "tensor": "ts"}


class Prog:

    def __init__(self, nc, ctx):
        self.nc = nc
        self.ctx = ctx
        self.q = {k: [] for k in QUEUES}
        self.cnt = {}
        self.hand = {}
        self.lastw = {k: {} for k in QUEUES}

    def sem(self, name):
        if name not in self.hand:
            self.hand[name] = self.ctx.enter_context(self.nc.semaphore(name))
            self.cnt[name] = 0
        return self.hand[name]

    def tok(self, name):
        self.sem(name)
        return (name, self.cnt[name])

    def emit(self, queue, fn, waits=(), inc=None, inc_by=1):
        ws = []
        for w in waits:
            if w is None:
                continue
            s, v = w
            if v <= 0:
                continue
            if self.lastw[queue].get(s, 0) >= v:
                continue
            self.lastw[queue][s] = v
            self.sem(s)
            ws.append((s, v))
        tk = None
        it = None
        if inc is not None:
            self.sem(inc)
            self.cnt[inc] += inc_by
            tk = (inc, self.cnt[inc])
            it = (inc, inc_by)
        self.q[queue].append((ws, fn, it))
        return tk

    def replay(self, queue, eng):
        for ws, fn, it in self.q[queue]:
            for s, v in ws:
                eng.wait_ge(self.hand[s], v)
            ins = fn(eng)
            if it is not None:
                assert ins is not None, f"{queue}: inc {it} but fn returned None"
                ins.then_inc(self.hand[it[0]], it[1])


class Bank:

    def __init__(self, pr, name, dtype=F32):
        width = 512 if dtype == F32 else 1024
        self.pr = pr
        self.t = pr.ctx.enter_context(pr.nc.psum_tensor(name, [P, width], dtype))
        self.w = None
        self.readers = []

    def mm_group(self, mms, extra_waits=(), cont=False, open_end=False):
        pr = self.pr
        n = len(mms)
        waits = [w for w in list(extra_waits) + ([] if cont else self.readers) if w is not None]
        tk = None
        for i, (o, l, r) in enumerate(mms):
            tk = pr.emit(
                "tensor",
                (lambda t, o=o, l=l, r=r, st=(i == 0 and not cont),
                 sp=(i == n - 1 and not open_end):
                 t.matmul(o, l, r, start=st, stop=sp)),
                waits=waits if i == 0 else (),
                inc="ts" if i == n - 1 else None)
        self.w = tk
        if not cont:
            self.readers = []
        return tk

    def read(self, queue, fn, extra_waits=()):
        tk = self.pr.emit(queue, fn, waits=[self.w] + list(extra_waits), inc=QSEM[queue])
        self.readers.append(tk)
        return tk


def ap(t, offset, dims):
    return bass.AP(t, offset, dims)


_BUILD_N = [0]


def build_program(debug=False, n_super=2 * NSUP, do_passb=True):
    _BUILD_N[0] += 1
    _tp = "tap_" if _BUILD_N[0] == 1 else f"tap{_BUILD_N[0]}_"
    nc = bass.Bass("TRN2", target_bir_lowering=False)

    d_xm = nc.dram_tensor("xm", [TOWN, D], F32, kind="ExternalInput")
    d_xp = nc.dram_tensor("xp", [TOWN, D], F32, kind="ExternalInput")
    d_gmask = nc.dram_tensor("gmask", [P, 2], F32, kind="ExternalInput")
    d_cvec = nc.dram_tensor("cvec", [D, 1], F32, kind="ExternalInput")
    d_wmod = nc.dram_tensor("wmod", [D, 6 * D], F32, kind="ExternalInput")
    d_thK = nc.dram_tensor("thK", [D, D], F32, kind="ExternalInput")
    d_thV = nc.dram_tensor("thV", [D, D], F32, kind="ExternalInput")
    d_thQ = nc.dram_tensor("thQ", [D, D], F32, kind="ExternalInput")
    d_w0T = nc.dram_tensor("w0T", [D, D], F32, kind="ExternalInput")
    d_b0 = nc.dram_tensor("b0r", [1, D], F32, kind="ExternalInput")
    d_rw = nc.dram_tensor("rw", [D, NE], F32, kind="ExternalInput")
    d_gw = nc.dram_tensor("gw", [NE, D, D], F32, kind="ExternalInput")
    d_uw = nc.dram_tensor("uw", [NE, D, D], F32, kind="ExternalInput")
    d_dw = nc.dram_tensor("dw", [NE, D, D], F32, kind="ExternalInput")
    d_sgw = nc.dram_tensor("sgw", [D, 2 * D], F32, kind="ExternalInput")
    d_suw = nc.dram_tensor("suw", [D, 2 * D], F32, kind="ExternalInput")
    d_sdw = nc.dram_tensor("sdw", [2 * D, D], F32, kind="ExternalInput")
    d_ident = nc.dram_tensor("identb", [P, P], BF16, kind="ExternalInput")
    d_mune = nc.dram_tensor("mune", [P, P], BF16, kind="ExternalInput")
    d_mlne = nc.dram_tensor("mlne", [P, P], BF16, kind="ExternalInput")
    d_muinc = nc.dram_tensor("muinc", [P, P], BF16, kind="ExternalInput")

    d_out = nc.dram_tensor("out", [TOWN, D], F32, kind="ExternalOutput")

    d_x2s = nc.dram_tensor("x2s", [TOWN, D], F32)
    d_h2s = nc.dram_tensor("h2s", [NCH * P, D], BF16)
    d_modscr = nc.dram_tensor("modscr", [1, 6 * D], F32)
    d_thq_scr = nc.dram_tensor("thq_scr", [D, D], BF16)
    d_gw_scr = nc.dram_tensor("gw_scr", [NE, P, DT * D], BF16)
    d_uw_scr = nc.dram_tensor("uw_scr", [NE, P, DT * D], BF16)
    d_dw_scr = nc.dram_tensor("dw_scr", [NE, P, DT * D], BF16)
    d_sg_scr = nc.dram_tensor("sg_scr", [2, P, DT * D], BF16)
    d_su_scr = nc.dram_tensor("su_scr", [2, P, DT * D], BF16)
    d_sd_scr = nc.dram_tensor("sd_scr", [2, P, DT * D], BF16)

    taps = {}

    with ExitStack() as octx:
        pr = Prog(nc, octx)
        E = octx.enter_context

        def tap(name, apv, waits):
            if not debug:
                return
            dt_ = nc.dram_tensor(_tp + name, list(apv.shape), apv.dtype, kind="ExternalOutput")
            taps[name] = True
            pr.emit("sync", lambda sy, a=apv, d2=dt_: sy.dma_start(d2[...], a),
                    waits=waits, inc="do", inc_by=16)

        s_ident = E(nc.sbuf_tensor("s_ident", [P, P], BF16))
        s_mune = E(nc.sbuf_tensor("s_mune", [P, P], BF16))
        s_mlne = E(nc.sbuf_tensor("s_mlne", [P, P], BF16))
        s_muinc = E(nc.sbuf_tensor("s_muinc", [P, P], BF16))
        ones_r1 = E(nc.sbuf_tensor("ones_r1", [1, P], BF16))
        ones_row = E(nc.sbuf_tensor("ones_row", [1, 512], BF16))
        ones_r1f = E(nc.sbuf_tensor("ones_r1f", [1, P], F32))
        ones_cl = E(nc.sbuf_tensor("ones_cl", [P, 1], BF16))
        onef_col = E(nc.sbuf_tensor("onef_col", [P, 1], F32))
        s_gmask = E(nc.sbuf_tensor("s_gmask", [P, 2], F32))
        gb_bc = E(nc.sbuf_tensor("gb_bc", [P, D], F32))
        ga_bc = E(nc.sbuf_tensor("ga_bc", [P, D], F32))
        comb_all = E(nc.sbuf_tensor("comb_all", [P, NCH * NE], F32))
        maskf = E(nc.sbuf_tensor("maskf", [P, NE], F32))
        mk0 = E(nc.sbuf_tensor("mk0", [P, NE], F32))
        tmp8 = E(nc.sbuf_tensor("tmp8", [P, NE], F32))
        b_f32 = E(nc.sbuf_tensor("b_f32", [1, D], F32))
        b_bfA = E(nc.sbuf_tensor("b_bfA", [1, D], BF16))
        b_bfB = E(nc.sbuf_tensor("b_bfB", [1, D], BF16))
        xb = E(nc.sbuf_tensor("xb", [P, 2 * D], F32))
        lnt = E(nc.sbuf_tensor("lnt", [P, 16], F32))
        hbf = E(nc.sbuf_tensor("hbf", [P, D], BF16))
        lg = E(nc.sbuf_tensor("lg", [P, NE], F32))
        zex = E(nc.sbuf_tensor("zex", [P, NE], F32))
        sc8 = E(nc.sbuf_tensor("sc8", [P, NE], F32))
        top8 = E(nc.sbuf_tensor("top8", [P, 8], F32))
        rcol = E(nc.sbuf_tensor("rcol", [P, 4], F32))
        h2stg = E(nc.sbuf_tensor("h2stg", [P, D], BF16))
        ca1c = E(nc.sbuf_tensor("ca1c", [P, DT], F32))
        cb1c = E(nc.sbuf_tensor("cb1c", [P, DT], F32))
        sa_f = E(nc.sbuf_tensor("sa_f", [P, DT], F32))
        sb_f = E(nc.sbuf_tensor("sb_f", [P, DT], F32))
        rw_bf = E(nc.sbuf_tensor("rw_bf", [P, DT * NE], BF16))

        bk = [Bank(pr, f"bk{i}") for i in range(7)]
        bkT = Bank(pr, "bkT", BF16)

        cdb = E(nc.sbuf_tensor("cdb", [P, 4], F32))
        pr.emit("gpsimd", lambda g: g.memset(cdb[:, 0:1], 2 * LR / D), inc="gs")
        pr.emit("gpsimd", lambda g: g.memset(cdb[:, 1:2], 1.0), inc="gs")
        pr.emit("gpsimd", lambda g: g.memset(cdb[:, 2:3], 0.0), inc="gs")
        pr.emit("gpsimd", lambda g: g.memset(cdb[:, 3:4], -1.0), inc="gs")
        nc.const_aps.register(F32, 2 * LR / D, cdb[:, 0:1])
        nc.const_aps.register(F32, 1.0, cdb[:, 1:2])
        nc.const_aps.register(F32, 0.0, cdb[:, 2:3])
        nc.const_aps.register(F32, -1.0, cdb[:, 3:4])

        for t_, d_ in ((s_ident, d_ident), (s_mune, d_mune), (s_mlne, d_mlne), (s_muinc, d_muinc)):
            pr.emit("sync", lambda sy, t2=t_, d2=d_: sy.dma_start(t2[:, :], d2[:, :]),
                    inc="dini", inc_by=16)
        pr.emit("sync", lambda sy: sy.dma_start(s_gmask[:, :], d_gmask[:, :]), inc="dini", inc_by=16)
        pr.emit("sync", lambda sy: sy.dma_start(b_f32[:, :], d_b0[:, :]), inc="dini", inc_by=16)
        pr.emit("gpsimd", lambda g: g.memset(ones_r1[:, :], 1.0), inc="gs")
        pr.emit("gpsimd", lambda g: g.memset(ones_row[:, :], 1.0), inc="gs")
        pr.emit("gpsimd", lambda g: g.memset(ones_cl[:, :], 1.0), inc="gs")
        pr.emit("gpsimd", lambda g: g.memset(ones_r1f[:, :], 1.0), inc="gs")
        pr.emit("gpsimd", lambda g: g.memset(onef_col[:, :], 1.0), inc="gs")
        t_dl0 = pr.tok("dini")
        t_ms0 = pr.tok("gs")

        t_bb0 = pr.emit("scalar", lambda s: s.copy(b_bfA[0:1, :], b_f32[0:1, :]),
                        waits=[t_dl0, t_ms0], inc="ss")

        wprep_toks = {}


        with ExitStack() as actx:
            EA = actx.enter_context

            with ExitStack() as wctx:
                wmb = wctx.enter_context(nc.sbuf_tensor("wmb", [P, 2 * 6 * D], F32))
                silc = wctx.enter_context(nc.sbuf_tensor("silc", [P, DT], F32))
                modr = wctx.enter_context(nc.sbuf_tensor("modr", [1, 6 * D], F32))
                tmpr = wctx.enter_context(nc.sbuf_tensor("tmpr", [1, 3 * D], F32))

                t_cv = pr.emit("sync", lambda sy: sy.dma_start(
                    silc[:, :], ap(d_cvec, 0, [[1, P], [P, DT]]),
                    allow_slow_non_contiguous=True), inc="dcv", inc_by=16)
                t_sil = pr.emit("scalar", lambda s: s.activation(silc[:, :], silc[:, :], AF.Silu),
                                waits=[t_cv], inc="ss")
                prev_mm = [None] * DT
                for pas in range(2):
                    mm_toks = []
                    for dt_i in range(DT):
                        slot = dt_i % 2
                        wl = [mm_toks[dt_i - 2]] if dt_i >= 2 else [prev_mm[DT - 2 + dt_i]]
                        tkd = pr.emit("sync", lambda sy, dt_i=dt_i, slot=slot: sy.dma_start(
                            ap(wmb, slot * 6 * D, [[2 * 6 * D, P], [1, 6 * D]]),
                            ap(d_wmod, dt_i * P * 6 * D, [[6 * D, P], [1, 6 * D]])),
                            waits=wl, inc=f"dwm{slot}", inc_by=16)
                        tk = None
                        for jj in range(6):
                            jb = pas * 6 + jj
                            b2 = bk[jj]
                            o = b2.t[0:1, 0:512]
                            l = silc[:, dt_i:dt_i + 1]
                            r = wmb[:, slot * 6 * D + jb * 512:slot * 6 * D + (jb + 1) * 512]
                            wl2 = [tkd]
                            if dt_i == 0:
                                wl2 += [t_sil] + b2.readers
                            tk = pr.emit("tensor",
                                         (lambda t, o=o, l=l, r=r, st=(dt_i == 0), sp=(dt_i == DT - 1):
                                          t.matmul(o, l, r, start=st, stop=sp)),
                                         waits=wl2, inc="ts" if jj == 5 else None)
                        mm_toks.append(tk)
                        if dt_i == DT - 1:
                            for jj in range(6):
                                bk[jj].w = tk
                                bk[jj].readers = []
                    for jj in range(6):
                        jb = pas * 6 + jj
                        bk[jj].read("vector", lambda v, jj=jj, jb=jb: v.tensor_copy(
                            modr[0:1, jb * 512:(jb + 1) * 512], bk[jj].t[0:1, 0:512]))
                    prev_mm = mm_toks
                t_mod = pr.tok("vs")

                t_gdone = t_mod
                for idx in (2, 5):
                    g_sl = modr[0:1, idx * D:(idx + 1) * D]
                    a_sl = tmpr[0:1, 0:D]
                    bb_sl = tmpr[0:1, D:2 * D]
                    s_sl = tmpr[0:1, 2 * D:3 * D]
                    pr.emit("vector", lambda v, g=g_sl, a=a_sl: v.tensor_scalar_max(a, g, 0.25),
                            waits=[t_mod, t_gdone], inc="vs")
                    pr.emit("vector", lambda v, g=g_sl, b3=bb_sl: v.tensor_scalar_min(b3, g, -0.25), inc="vs")
                    tc0 = pr.emit("vector", lambda v, g=g_sl, s2=s_sl: v.tensor_single_scalar(s2, g, 0.0, op=ALU.is_ge), inc="vs")
                    tc1 = pr.emit("vector", lambda v, a=a_sl, b3=bb_sl: v.tensor_sub(a, a, b3),
                                  waits=[tc0], inc="vs")
                    tc2 = pr.emit("vector", lambda v, a=a_sl, s2=s_sl: v.tensor_mul(a, a, s2),
                                  waits=[tc1], inc="vs")
                    t_gdone = pr.emit("vector", lambda v, g=g_sl, a=a_sl, b3=bb_sl: v.tensor_add(g, b3, a),
                                      waits=[tc2], inc="vs")
                t_clamp = pr.tok("vs")

                for row_i, dst in ((2, ga_bc), (5, gb_bc)):
                    for jh in range(2):
                        b2 = bk[6]
                        b2.mm_group(
                            [(b2.t[:, 0:512], ones_r1f[:, :],
                              modr[0:1, row_i * D + jh * 512:row_i * D + (jh + 1) * 512])],
                            extra_waits=[t_clamp, t_ms0])
                        b2.read("vector", lambda v, dst=dst, jh=jh, b2=b2: v.tensor_copy(
                            dst[:, jh * 512:(jh + 1) * 512], b2.t[:, 0:512]))
                t_ms = pr.emit("sync", lambda sy: sy.dma_start(d_modscr[:, :], modr[:, :]),
                               waits=[t_clamp], inc="dms", inc_by=16)
                t_l1 = pr.emit("sync", lambda sy: sy.dma_start(
                    ca1c[:, :], ap(d_modscr, 1 * D, [[1, P], [P, DT]]),
                    allow_slow_non_contiguous=True), waits=[t_ms], inc="dca", inc_by=16)
                t_l2 = pr.emit("sync", lambda sy: sy.dma_start(
                    cb1c[:, :], ap(d_modscr, 4 * D, [[1, P], [P, DT]]),
                    allow_slow_non_contiguous=True), waits=[t_ms], inc="dcb", inc_by=16)
                t_l3 = pr.emit("gpsimd", lambda g: g.dma_start(
                    sa_f[:, :], ap(d_modscr, 0 * D, [[1, P], [P, DT]]),
                    allow_slow_non_contiguous=True), waits=[t_ms], inc="dsa", inc_by=16)
                t_l4 = pr.emit("gpsimd", lambda g: g.dma_start(
                    sb_f[:, :], ap(d_modscr, 3 * D, [[1, P], [P, DT]]),
                    allow_slow_non_contiguous=True), waits=[t_ms], inc="dsb2", inc_by=16)
                t_c1 = pr.emit("vector", lambda v: v.tensor_scalar_add(ca1c[:, :], ca1c[:, :], 1.0),
                               waits=[t_l1], inc="vs")
                t_c2 = pr.emit("vector", lambda v: v.tensor_scalar_add(cb1c[:, :], cb1c[:, :], 1.0),
                               waits=[t_l2], inc="vs")
                t_wmod_ts = pr.tok("ts")
                t_wmod_vs = pr.tok("vs")

            thK = EA(nc.sbuf_tensor("s_thK", [P, DT * D], BF16))
            thVn = EA(nc.sbuf_tensor("s_thVn", [P, DT * D], BF16))
            thQp = EA(nc.sbuf_tensor("s_thQp", [P, DT * D], BF16))
            WT = EA(nc.sbuf_tensor("WT", [P, DT * D], F32))
            Wbf = EA(nc.sbuf_tensor("Wbf", [P, DT * D], BF16))
            hft = EA(nc.sbuf_tensor("hft", [P, 2 * DT * 512], BF16))
            Kft = EA(nc.sbuf_tensor("Kft", [P, DT * 512], BF16))
            Ktk = EA(nc.sbuf_tensor("Ktk", [P, SUP * D], BF16))
            Qft = EA(nc.sbuf_tensor("Qft", [P, DT * 512], BF16))
            Gsb = EA(nc.sbuf_tensor("Gsb", [P, SUP * D], BF16))
            Scr = EA(nc.sbuf_tensor("Scr", [P, 6 * P], BF16))
            Sqr = EA(nc.sbuf_tensor("Sqr", [P, 10 * P], BF16))
            Usb = EA(nc.sbuf_tensor("Usb", [P, 8 * P], BF16))
            usb_rd = {j: None for j in range(8)}
            Bp = EA(nc.sbuf_tensor("Bp", [P, 2 * SUP * P], BF16))
            Btp = EA(nc.sbuf_tensor("Btp", [P, 2 * SUP * P], BF16))
            Dsb = EA(nc.sbuf_tensor("Dsb", [P, SUP * P], BF16))
            Xsb = EA(nc.sbuf_tensor("Xsb", [P, SUP * P], BF16))
            RHSs = EA(nc.sbuf_tensor("RHSs", [P, D], BF16))
            x2b = EA(nc.sbuf_tensor("x2b", [P, 2 * D], F32))
            xeb = EA(nc.sbuf_tensor("xeb", [P, 2 * D], F32))

            for dt_i in range(DT):
                pr.emit("gpsimd", lambda g, dt_i=dt_i: g.dma_start(
                    ap(thK, dt_i * D, [[DT * D, P], [1, D]]),
                    ap(d_thK, dt_i * P * D, [[D, P], [1, D]])),
                    waits=[t_wmod_ts, t_wmod_vs], inc="dthk", inc_by=16)
            t_thk_all = pr.tok("dthk")
            thk_toks = [t_thk_all] * DT
            for dt_i in range(DT):
                pr.emit("gpsimd", lambda g, dt_i=dt_i: g.dma_start(
                    ap(thVn, dt_i * D, [[DT * D, P], [1, D]]),
                    ap(d_thV, dt_i * P * D, [[D, P], [1, D]])),
                    waits=[t_wmod_ts, t_wmod_vs], inc="dthv", inc_by=16)
            t_thv_all = pr.tok("dthv")
            thv_toks = [t_thv_all] * DT
            for dt_i in range(DT):
                pr.emit("gpsimd", lambda g, dt_i=dt_i: g.dma_start(
                    ap(thQp, dt_i * D, [[DT * D, P], [1, D]]),
                    ap(d_thQ, dt_i * P * D, [[D, P], [1, D]])),
                    waits=[t_wmod_ts, t_wmod_vs], inc="dthq", inc_by=16)
            t_thq_all = pr.tok("dthq")
            thq_toks = [t_thq_all] * DT

            for dt_i in range(DT):
                pr.emit("scalar", lambda s, dt_i=dt_i: s.activation(
                    thVn[:, dt_i * D:(dt_i + 1) * D], thVn[:, dt_i * D:(dt_i + 1) * D],
                    AF.Identity, scale=-1.0),
                    waits=[thv_toks[dt_i]], inc="ss")
            t_thprime = pr.tok("ss")
            t_qprime = t_thq_all

            t_rwl = pr.emit("gpsimd", lambda g: g.dma_start(
                rw_bf[:, :], ap(d_rw, 0, [[NE, P], [P * NE, DT], [1, NE]])),
                waits=[t_wmod_ts, t_wmod_vs], inc="drw", inc_by=16)


            def wprep_family(name, pairs):
                for s_ap, d_ap in pairs:
                    pr.emit("gpsimd", lambda g, s2=s_ap, d2=d_ap: g.dma_start(d2, s2),
                            inc="dp", inc_by=16)
                wprep_toks[name] = pr.tok("dp")

            def gu_pairs(src, dst):
                out = []
                for e2 in range(NE):
                    for ht in range(DT):
                        out.append((ap(src, e2 * D * D + ht * P * D, [[D, P], [1, D]]),
                                    ap(dst, e2 * P * DT * D + ht * D, [[DT * D, P], [1, D]])))
                return out
            wprep_family("gw", gu_pairs(d_gw, d_gw_scr))
            wprep_family("uw", gu_pairs(d_uw, d_uw_scr))
            wprep_family("dw", gu_pairs(d_dw, d_dw_scr))

            def sh_pairs(src, dst):
                out = []
                for fh in range(2):
                    for ht in range(DT):
                        out.append((ap(src, ht * P * 2 * D + fh * D, [[2 * D, P], [1, D]]),
                                    ap(dst, fh * P * DT * D + ht * D, [[DT * D, P], [1, D]])))
                return out
            wprep_family("sgw", sh_pairs(d_sgw, d_sg_scr))
            wprep_family("suw", sh_pairs(d_suw, d_su_scr))
            sd_pairs = []
            for fh in range(2):
                for fj in range(DT):
                    sd_pairs.append((ap(d_sdw, (fh * D + fj * P) * D, [[D, P], [1, D]]),
                                     ap(d_sd_scr, fh * P * DT * D + fj * D, [[DT * D, P], [1, D]])))
            wprep_family("sdw", sd_pairs)
            dp_all = pr.tok("dp")
            for k2 in ("gw", "uw", "dw", "sgw", "suw", "sdw"):
                wprep_toks[k2] = dp_all

            for dt_i in range(DT):
                pr.emit("sync", lambda sy, dt_i=dt_i: sy.dma_start(
                    ap(WT, dt_i * D, [[DT * D, P], [1, D]]),
                    ap(d_w0T, dt_i * P * D, [[D, P], [1, D]])),
                    waits=[t_wmod_ts, t_wmod_vs], inc="dw0t", inc_by=16)
            t_w0_all = pr.tok("dw0t")
            for dt_i in range(DT):
                pr.emit("scalar", lambda s, dt_i=dt_i: s.copy(
                    Wbf[:, dt_i * D:(dt_i + 1) * D], WT[:, dt_i * D:(dt_i + 1) * D]),
                    waits=[t_w0_all], inc="ss")
            t_w0 = pr.tok("ss")

            t_prologue = [pr.tok("ts"), pr.tok("vs"), pr.tok("ss"), pr.tok("gs"),
                          t_w0, t_bb0, t_thprime, t_qprime,
                          t_thk_all, t_thq_all, t_rwl, t_l3, t_l4]

            ln_done = {}
            x_dma = {}
            x2_spill_tok = None
            h2_guard = [None, None]
            last_cp = [None]
            last_hn = [None]
            last_tr = [None]

            def x_src(s_abs, ci):
                d_src = d_xp if s_abs < NSUP else d_xm
                c16 = (s_abs % NSUP) * SUP + ci
                return ap(d_src, c16 * P * D, [[D, P], [1, D]])

            def ensure_xdma(cl):
                if cl >= 2 * NSUP * SUP or cl in x_dma:
                    return
                slot = cl % 2
                wl = list(t_prologue) if cl == 0 else []
                if cl - 2 >= 0:
                    wl += list(ln_done[cl - 2])
                x_dma[cl] = pr.emit("sync", lambda sy, cl=cl, slot=slot: sy.dma_start(
                    ap(xb, slot * D, [[2 * D, P], [1, D]]), x_src(cl // SUP, cl % SUP)),
                    waits=wl, inc=f"dx{slot}", inc_by=16)

            def pair_id_s(c, cp):
                return c * (c - 1) // 2 + cp

            def pair_id_q(c, cp):
                return c * (c + 1) // 2 + cp

            def emit_ln(x_ap, wait0, hazard_guard):
                pr.emit("vector", lambda v, x3=x_ap: v.bn_stats(lnt[:, 0:6], x3[:, 0:512]),
                        waits=wait0 + [last_hn[0]], inc="vs")
                t_b2 = pr.emit("vector", lambda v, x3=x_ap: v.bn_stats(lnt[:, 6:12], x3[:, 512:1024]), inc="vs")
                t_ag = pr.emit("vector", lambda v: v.bn_aggr(lnt[:, 12:14], lnt[:, 0:12]),
                               waits=[t_b2], inc="vs")
                t_v1 = pr.emit("vector", lambda v: v.tensor_scalar_add(lnt[:, 14:15], lnt[:, 13:14], EPS),
                               waits=[t_ag], inc="vs")
                t_s1 = pr.emit("scalar", lambda s: s.activation(lnt[:, 14:15], lnt[:, 14:15], AF.Sqrt),
                               waits=[t_v1], inc="ss")
                t_rc = pr.emit("vector", lambda v: v.reciprocal(lnt[:, 15:16], lnt[:, 14:15]),
                               waits=[t_s1], inc="vs")
                t_v3 = pr.emit("vector", lambda v: v.scalar_tensor_tensor(
                    lnt[:, 14:15], lnt[:, 12:13], -1.0, lnt[:, 15:16], op0=ALU.mult, op1=ALU.mult),
                    waits=[t_rc], inc="vs")
                t_hn = pr.emit("scalar", lambda s, x3=x_ap: s.activation(
                    hbf[:, :], x3, AF.Identity, bias=lnt[:, 14:15], scale=lnt[:, 15:16]),
                    waits=[t_v3] + hazard_guard, inc="ss")
                last_hn[0] = t_hn
                return t_v3, t_hn

            def emit_transposes(t_hn, dst_fns, extra=()):
                tk = None
                for i in range(DT):
                    tk = pr.emit("tensor",
                                 (lambda t, i=i: t.transpose(
                                     bkT.t[:, i * P:(i + 1) * P], hbf[:, i * P:(i + 1) * P],
                                     s_ident[:, :])),
                                 waits=([t_hn] + bkT.readers) if i == 0 else (),
                                 inc="ts" if i == DT - 1 else None)
                bkT.w = tk
                bkT.readers = []
                last_tr[0] = tk
                tk = None
                for fi, fn in enumerate(dst_fns):
                    tk = bkT.read("scalar", fn, extra_waits=list(extra) if fi == 0 else ())
                return tk

            def emit_state_update(t_chain, t_outs, npar_buf):
                for dt_i in range(DT):
                    for jh in range(2):
                        b2 = bk[(dt_i * 2 + jh) % 2]
                        mms = []
                        for c in range(SUP):
                            mms.append((b2.t[:, 0:512],
                                        Ktk[:, c * D + dt_i * P:c * D + (dt_i + 1) * P],
                                        Gsb[:, c * D + jh * 512:c * D + (jh + 1) * 512]))
                        b2.mm_group(mms, extra_waits=[t_chain])
                        b2.read("vector", lambda v, b2=b2, dt_i=dt_i, jh=jh: v.tensor_add(
                            WT[:, dt_i * D + jh * 512:dt_i * D + (jh + 1) * 512],
                            WT[:, dt_i * D + jh * 512:dt_i * D + (jh + 1) * 512],
                            b2.t[:, 0:512]))
                t_wupd = pr.tok("vs")
                for dt_i in range(DT):
                    pr.emit("scalar", lambda s, dt_i=dt_i: s.copy(
                        Wbf[:, dt_i * D:(dt_i + 1) * D], WT[:, dt_i * D:(dt_i + 1) * D]),
                        waits=[t_wupd, t_outs], inc="ss")
                for jh in range(2):
                    b2 = bk[2 + jh]
                    mms = []
                    for c in range(SUP):
                        mms.append((b2.t[0:1, 0:512], ones_cl[:, :],
                                    Gsb[:, c * D + jh * 512:c * D + (jh + 1) * 512]))
                    b2.mm_group(mms, extra_waits=[t_chain])
                    b2.read("vector", lambda v, b2=b2, jh=jh: v.tensor_add(
                        b_f32[0:1, jh * 512:(jh + 1) * 512],
                        b_f32[0:1, jh * 512:(jh + 1) * 512], b2.t[0:1, 0:512]))
                t_bupd = pr.tok("vs")
                pr.emit("scalar", lambda s, nb=npar_buf: s.copy(nb[0:1, :], b_f32[0:1, :]),
                        waits=[t_bupd, t_outs], inc="ss")

            pending_state = None
            t_hft_map = {}

            def emit_lnblock(s2, guard):
                hoff = (s2 % 2) * DT * 512
                for ci in range(SUP):
                    cl = s2 * SUP + ci
                    ensure_xdma(cl)
                    ensure_xdma(cl + 1)
                    slot = cl % 2
                    xsl = xb[:, slot * D:(slot + 1) * D]
                    t_v3, t_hn = emit_ln(xsl, [x_dma[cl]], [last_tr[0], last_cp[0]])
                    ln_done[cl] = (t_v3, t_hn)
                    ensure_xdma(cl + 2)
                    emit_transposes(t_hn, [
                        (lambda s, ci=ci, dt=dt, hoff=hoff: s.activation(
                            hft[:, hoff + dt * 512 + ci * P:hoff + dt * 512 + ci * P + P],
                            bkT.t[:, dt * P:(dt + 1) * P], AF.Identity,
                            scale=ca1c[:, dt:dt + 1], bias=sa_f[:, dt:dt + 1]))
                        for dt in range(DT)],
                        extra=[guard])
                t_hft_map[s2] = pr.tok("ss")

            for s_abs in range(n_super):
                is_m = s_abs >= NSUP
                hoff = (s_abs % 2) * DT * 512
                par_buf = b_bfA if s_abs % 2 == 0 else b_bfB
                npar_buf = b_bfB if s_abs % 2 == 0 else b_bfA
                gcol = 1 if is_m else 0
                t_guard_a = pr.tok("ts")
                if pending_state is not None:
                    emit_state_update(*pending_state)
                    pending_state = None
                t_super_ts0 = pr.tok("ts")

                if s_abs == 0:
                    emit_lnblock(0, t_guard_a)
                t_hft = t_hft_map[s_abs]

                for et in range(DT):
                    b2 = bk[2 + et % 2]
                    mms = []
                    for dt_i in range(DT):
                        mms.append((b2.t[:, 0:512],
                                    thK[:, dt_i * D + et * P:dt_i * D + (et + 1) * P],
                                    hft[:, hoff + dt_i * 512:hoff + dt_i * 512 + 512]))
                    b2.mm_group(mms, extra_waits=[t_hft])
                    b2.read("scalar", lambda s, et=et, b2=b2: s.copy(
                        Kft[:, et * 512:(et + 1) * 512], b2.t[:, 0:512]),
                        extra_waits=[t_super_ts0])
                if is_m:
                    for et in range(DT):
                        b2 = bk[4 + et % 2]
                        mms = []
                        for dt_i in range(DT):
                            mms.append((b2.t[:, 0:512],
                                        thQp[:, dt_i * D + et * P:dt_i * D + (et + 1) * P],
                                        hft[:, hoff + dt_i * 512:hoff + dt_i * 512 + 512]))
                        b2.mm_group(mms, extra_waits=[t_hft])
                        b2.read("scalar", lambda s, et=et, b2=b2: s.copy(
                            Qft[:, et * 512:(et + 1) * 512], b2.t[:, 0:512]),
                            extra_waits=[t_super_ts0])
                t_proj = [pr.tok("ss"), pr.tok("vs")]

                for cp in range(SUP):
                    b2 = bk[cp % 2]
                    mms = []
                    for et in range(DT):
                        mms.append((b2.t[:, 0:512],
                                    Kft[:, et * 512 + cp * P:et * 512 + (cp + 1) * P],
                                    Kft[:, et * 512:et * 512 + 512]))
                    b2.mm_group(mms, extra_waits=t_proj)
                    usl = Usb[:, cp * P:(cp + 1) * P]
                    t_u = b2.read("scalar", lambda s, b2=b2, cp=cp, usl=usl: s.activation(
                        usl, b2.t[:, cp * P:(cp + 1) * P], AF.Identity,
                        scale=2 * LR / D, bias=2 * LR / D),
                        extra_waits=[usb_rd[cp]])
                    pr.emit("vector", lambda v, cp=cp, usl=usl: v.tensor_mul(
                        Bp[:, cp * P:(cp + 1) * P], usl, s_mune[:, :]),
                        waits=[t_u, t_super_ts0], inc="vs")
                    usb_rd[cp] = pr.emit("vector", lambda v, cp=cp, usl=usl: v.tensor_mul(
                        Btp[:, cp * P:(cp + 1) * P], usl, s_mlne[:, :]),
                        waits=[t_u], inc="vs")
                    for c in range(cp + 1, SUP):
                        pid = pair_id_s(c, cp)
                        b2.read("scalar", lambda s, b2=b2, pid=pid, c=c: s.activation(
                            Scr[:, pid * P:(pid + 1) * P], b2.t[:, c * P:(c + 1) * P],
                            AF.Identity, scale=1.0, bias=1.0),
                            extra_waits=[t_super_ts0])
                if is_m:
                    for cp in range(SUP):
                        b2 = bk[2 + cp % 2]
                        mms = []
                        for et in range(DT):
                            mms.append((b2.t[:, 0:512],
                                        Kft[:, et * 512 + cp * P:et * 512 + (cp + 1) * P],
                                        Qft[:, et * 512:et * 512 + 512]))
                        b2.mm_group(mms, extra_waits=t_proj)
                        pid = pair_id_q(cp, cp)
                        usl = Usb[:, (4 + cp) * P:(5 + cp) * P]
                        t_u = b2.read("scalar", lambda s, b2=b2, cp=cp, usl=usl: s.activation(
                            usl, b2.t[:, cp * P:(cp + 1) * P], AF.Identity, scale=1.0, bias=1.0),
                            extra_waits=[usb_rd[4 + cp]])
                        usb_rd[4 + cp] = pr.emit("vector", lambda v, pid=pid, usl=usl: v.tensor_mul(
                            Sqr[:, pid * P:(pid + 1) * P], usl, s_muinc[:, :]),
                            waits=[t_u, t_super_ts0], inc="vs")
                        for c in range(cp + 1, SUP):
                            pid = pair_id_q(c, cp)
                            b2.read("scalar", lambda s, b2=b2, pid=pid, c=c: s.activation(
                                Sqr[:, pid * P:(pid + 1) * P], b2.t[:, c * P:(c + 1) * P],
                                AF.Identity, scale=1.0, bias=1.0),
                                extra_waits=[t_super_ts0])
                t_pairs = [pr.tok("ss"), pr.tok("vs")]

                for c in range(SUP):
                    tk = None
                    for et in range(DT):
                        tk = pr.emit("tensor",
                                     (lambda t, c=c, et=et: t.transpose(
                                         bkT.t[:, et * P:(et + 1) * P],
                                         Kft[:, et * 512 + c * P:et * 512 + c * P + P],
                                         s_ident[:, :])),
                                     waits=(t_proj + bkT.readers) if et == 0 else (),
                                     inc="ts" if et == DT - 1 else None)
                    bkT.w = tk
                    bkT.readers = []
                    bkT.read("scalar", lambda s, c=c: s.copy(
                        Ktk[:, c * D:(c + 1) * D], bkT.t[:, :]))

                if s_abs + 1 < n_super:
                    emit_lnblock(s_abs + 1, t_super_ts0)

                x_tok = {}
                for c in range(SUP):
                    x_tok[c] = pr.emit("vector", lambda v, c=c: v.tensor_add(
                        Xsb[:, c * P:(c + 1) * P], s_ident[:, :], Bp[:, c * P:(c + 1) * P]),
                        waits=t_pairs + [t_super_ts0], inc="vs")
                ds_free = {c: None for c in range(SUP)}
                for m in range(1, 7):
                    pslot = (m - 1) % 2
                    cslot = m % 2
                    last = (m == 6)
                    bt_tok = {}
                    if not last:
                        for c in range(SUP):
                            bsl = Bp[:, (pslot * SUP + c) * P:(pslot * SUP + c + 1) * P]
                            btsl = Btp[:, (pslot * SUP + c) * P:(pslot * SUP + c + 1) * P]
                            b2 = bk[c % 2]
                            b2.mm_group([(b2.t[:, 0:P], btsl, bsl)],
                                        extra_waits=[x_tok[c]] if m == 1 else [])
                            b2.read("scalar", lambda s, b2=b2, c=c, cslot=cslot: s.copy(
                                Bp[:, (cslot * SUP + c) * P:(cslot * SUP + c + 1) * P], b2.t[:, 0:P]))
                    for c in range(SUP):
                        bsl = Bp[:, (pslot * SUP + c) * P:(pslot * SUP + c + 1) * P]
                        btsl = Btp[:, (pslot * SUP + c) * P:(pslot * SUP + c + 1) * P]
                        b3 = bk[2 + c % 2]
                        b3.mm_group([(b3.t[:, 0:P], bsl, btsl)],
                                    extra_waits=[x_tok[c]] if m == 1 else [])
                        bt_tok[c] = b3.read("scalar", lambda s, b3=b3, c=c, cslot=cslot: s.copy(
                            Btp[:, (cslot * SUP + c) * P:(cslot * SUP + c + 1) * P], b3.t[:, 0:P]))
                    for c in range(SUP):
                        b4 = bk[4 + c % 2]
                        b4.mm_group([(b4.t[:, 0:P],
                                      Btp[:, (cslot * SUP + c) * P:(cslot * SUP + c + 1) * P],
                                      Xsb[:, c * P:(c + 1) * P])],
                                    extra_waits=[bt_tok[c], x_tok[c]])
                        t_d = b4.read("scalar", lambda s, b4=b4, c=c: s.copy(
                            Dsb[:, c * P:(c + 1) * P], b4.t[:, 0:P]),
                            extra_waits=[ds_free[c]])
                        x_tok[c] = pr.emit("vector", lambda v, c=c: v.tensor_add(
                            Xsb[:, c * P:(c + 1) * P], Xsb[:, c * P:(c + 1) * P],
                            Dsb[:, c * P:(c + 1) * P]),
                            waits=[t_d], inc="vs")
                        ds_free[c] = x_tok[c]
                t_newton = pr.tok("vs")

                g_tok = {}
                ep_pend = None

                def emit_epilogue(c, t_xe, t_oap=()):
                    nonlocal x2_spill_tok, h2_guard
                    c16 = (s_abs % NSUP) * SUP + c
                    sl = (c % 2) * D
                    x2sl = x2b[:, sl:sl + D]
                    t_x2m = pr.emit("vector", lambda v, sl=sl: v.tensor_add(
                        x2b[:, sl:sl + D], x2b[:, sl:sl + D], xeb[:, sl:sl + D]),
                        waits=[t_xe] + list(t_oap), inc="vs")
                    x2_spill_tok = pr.emit("sync", lambda sy, c16=c16, x2sl=x2sl: sy.dma_start(
                        ap(d_x2s, c16 * P * D, [[D, P], [1, D]]), x2sl),
                        waits=[t_x2m], inc="dsp", inc_by=16)
                    t_v3, t_hn = emit_ln(x2sl, [t_x2m], [last_tr[0], last_cp[0]])
                    t_h2 = emit_transposes(t_hn, [
                        (lambda s, dt=dt: s.activation(
                            h2stg[:, dt * P:(dt + 1) * P],
                            bkT.t[:, dt * P:(dt + 1) * P], AF.Identity,
                            scale=cb1c[:, dt:dt + 1], bias=sb_f[:, dt:dt + 1]))
                        for dt in range(DT)],
                        extra=[h2_guard[0], h2_guard[1]])
                    h2_spill = pr.emit("sync", lambda sy, c16=c16: sy.dma_start(
                        ap(d_h2s, c16 * P * D, [[D, P], [1, D]]), h2stg[:, :]),
                        waits=[t_h2], inc="dh2s", inc_by=16)
                    b6 = bk[4]
                    mms = []
                    for dt_i in range(DT):
                        mms.append((b6.t[:, 0:NE],
                                    h2stg[:, dt_i * P:(dt_i + 1) * P],
                                    rw_bf[:, dt_i * NE:(dt_i + 1) * NE]))
                    b6.mm_group(mms, extra_waits=[t_h2])
                    h2_guard = [h2_spill, pr.tok("ts")]
                    t_lg = b6.read("vector", lambda v, b6=b6: v.tensor_copy(lg[:, :], b6.t[:, 0:NE]))
                    t_m1 = pr.emit("vector", lambda v: v.reduce_max(rcol[:, 0:1], lg[:, :], axis=AX.X),
                                   waits=[t_lg], inc="vs")
                    t_m2 = pr.emit("vector", lambda v: v.tensor_scalar_mul(rcol[:, 1:2], rcol[:, 0:1], -1.0),
                                   waits=[t_m1], inc="vs")
                    t_e1 = pr.emit("scalar", lambda s: s.activation(
                        zex[:, :], lg[:, :], AF.Exp, bias=rcol[:, 1:2]),
                        waits=[t_m2], inc="ss")
                    t_r1 = pr.emit("vector", lambda v: v.reduce_sum(rcol[:, 2:3], zex[:, :], axis=AX.X),
                                   waits=[t_e1], inc="vs")
                    t_r2 = pr.emit("vector", lambda v: v.reciprocal(rcol[:, 3:4], rcol[:, 2:3]),
                                   waits=[t_r1], inc="vs")
                    t_sc = pr.emit("scalar", lambda s: s.activation(
                        sc8[:, :], zex[:, :], AF.Identity, scale=rcol[:, 3:4]),
                        waits=[t_r2], inc="ss")
                    t_t1 = pr.emit("vector", lambda v: v.reduce_max(
                        top8[:, 0:1], sc8[:, :], axis=AX.X), waits=[t_sc], inc="vs")
                    t_m0 = pr.emit("vector", lambda v: v.scalar_tensor_tensor(
                        mk0[:, :], sc8[:, :], top8[:, 0:1], sc8[:, :],
                        op0=ALU.is_ge, op1=ALU.bypass), waits=[t_t1], inc="vs")
                    t_pm = pr.emit("vector", lambda v: v.tensor_mul(
                        maskf[:, :], mk0[:, :], sc8[:, :]), waits=[t_m0], inc="vs")
                    t_sb2 = pr.emit("vector", lambda v: v.tensor_sub(
                        maskf[:, :], sc8[:, :], maskf[:, :]), waits=[t_pm], inc="vs")
                    t_t2 = pr.emit("vector", lambda v: v.reduce_max(
                        top8[:, 1:2], maskf[:, :], axis=AX.X), waits=[t_sb2], inc="vs")
                    t_mx = pr.emit("vector", lambda v: v.scalar_tensor_tensor(
                        tmp8[:, :], maskf[:, :], top8[:, 1:2], maskf[:, :],
                        op0=ALU.is_ge, op1=ALU.bypass), waits=[t_t2], inc="vs")
                    t_msum = pr.emit("vector", lambda v: v.tensor_add(
                        maskf[:, :], mk0[:, :], tmp8[:, :]), waits=[t_mx], inc="vs")
                    pr.emit("vector", lambda v, c16=c16: v.tensor_mul(
                        comb_all[:, c16 * NE:(c16 + 1) * NE], maskf[:, :], sc8[:, :]),
                        waits=[t_msum], inc="vs")

                for c in range(SUP):
                    bl, bh = (bk[0], bk[1]) if c % 2 == 0 else (bk[2], bk[3])
                    if is_m:
                        t_xe = pr.emit("sync", lambda sy, s_abs=s_abs, c=c: sy.dma_start(
                            xeb[:, (c % 2) * D:(c % 2 + 1) * D], x_src(s_abs, c)),
                            waits=[pr.tok("vs")], inc=f"dxe{c % 2}", inc_by=16)
                    for jh, b2 in ((0, bl), (1, bh)):
                        mms = []
                        for et in range(DT):
                            mms.append((b2.t[:, 0:512],
                                        Kft[:, et * 512 + c * P:et * 512 + (c + 1) * P],
                                        Wbf[:, et * D + jh * 512:et * D + (jh + 1) * 512]))
                        for dt_i in range(DT):
                            mms.append((b2.t[:, 0:512],
                                        hft[:, hoff + dt_i * 512 + c * P:hoff + dt_i * 512 + (c + 1) * P],
                                        thVn[:, dt_i * D + jh * 512:dt_i * D + (jh + 1) * 512]))
                        mms.append((b2.t[:, 0:512], ones_r1[:, :],
                                    par_buf[0:1, jh * 512:(jh + 1) * 512]))
                        b2.mm_group(mms, extra_waits=[t_newton], open_end=(c > 0))
                    if is_m:
                        for jh, b2 in ((0, bk[5]), (1, bk[6])):
                            mms = []
                            for et in range(DT):
                                mms.append((b2.t[:, 0:512],
                                            Qft[:, et * 512 + c * P:et * 512 + (c + 1) * P],
                                            Wbf[:, et * D + jh * 512:et * D + (jh + 1) * 512]))
                            mms.append((b2.t[:, 0:512], ones_r1[:, :],
                                        par_buf[0:1, jh * 512:(jh + 1) * 512]))
                            b2.mm_group(mms, extra_waits=[t_newton], open_end=True)
                    for jh, b2 in ((0, bl), (1, bh)):
                        mms = []
                        for cp in range(c):
                            pid = pair_id_s(c, cp)
                            mms.append((b2.t[:, 0:512], Scr[:, pid * P:(pid + 1) * P],
                                        Gsb[:, cp * D + jh * 512:cp * D + (jh + 1) * 512]))
                        if mms:
                            b2.mm_group(mms, extra_waits=[g_tok[c - 1]], cont=True)
                        b2.read("scalar", lambda s, b2=b2, jh=jh: s.copy(
                            RHSs[0:P, jh * 512:(jh + 1) * 512], b2.t[:, 0:512]))
                    t_rhs = pr.tok("ss")
                    if is_m and ep_pend is not None:
                        emit_epilogue(*ep_pend)
                        ep_pend = None
                    bg = bk[4]
                    for jh in range(2):
                        bg.mm_group([(bg.t[:, 0:512], Xsb[:, c * P:(c + 1) * P],
                                      RHSs[:, jh * 512:(jh + 1) * 512])],
                                    extra_waits=[t_rhs])
                        g_tok[c] = bg.read("scalar", lambda s, b5=bg, c=c, jh=jh, gcol=gcol: s.activation(
                            Gsb[:, c * D + jh * 512:c * D + (jh + 1) * 512], b5.t[:, 0:512],
                            AF.Identity, scale=s_gmask[:, gcol:gcol + 1]),
                            extra_waits=[t_super_ts0])
                    if is_m:
                        t_oap = []
                        for jh, b2 in ((0, bk[5]), (1, bk[6])):
                            mms = []
                            for cp in range(c + 1):
                                pid = pair_id_q(c, cp)
                                mms.append((b2.t[:, 0:512], Sqr[:, pid * P:(pid + 1) * P],
                                            Gsb[:, cp * D + jh * 512:cp * D + (jh + 1) * 512]))
                            b2.mm_group(mms, extra_waits=[g_tok[c]], cont=True)
                            t_oap.append(b2.read("vector", lambda v, b2=b2, jh=jh, c=c: v.tensor_mul(
                                x2b[:, (c % 2) * D + jh * 512:(c % 2) * D + (jh + 1) * 512],
                                b2.t[:, 0:512],
                                ga_bc[:, jh * 512:(jh + 1) * 512]),
                                extra_waits=[pr.tok("ss"), x2_spill_tok]))
                        ep_pend = (c, t_xe, t_oap)
                if is_m and ep_pend is not None:
                    emit_epilogue(*ep_pend)
                    ep_pend = None
                t_chain = pr.tok("ss")

                t_outs = pr.tok("ts")
                pending_state = (t_chain, t_outs, npar_buf)

            if pending_state is not None:
                emit_state_update(*pending_state)
                pending_state = None

            t_phaseA = [pr.tok("ts"), pr.tok("vs"), pr.tok("ss"), pr.tok("dsp"), pr.tok("dh2s"),
                        pr.tok("dx0"), pr.tok("dx1"), pr.tok("dxe0"), pr.tok("dxe1")]

            if debug:
                tap("WT", WT[:, :], t_phaseA)
                tap("Gsb", Gsb[:, :], t_phaseA)
                tap("bf32", b_f32[:, :], t_phaseA)
                tap("comb", comb_all[:, :], t_phaseA)
                tap("Kft", Kft[:, :], t_phaseA)
                tap("Ktk", Ktk[:, :], t_phaseA)
                tap("Qft", Qft[:, :], t_phaseA)
                tap("Xsb", Xsb[:, :], t_phaseA)
                tap("hft", hft[:, :], t_phaseA)
                tap("modcols_ca1", ca1c[:, :], t_phaseA)
                tap("h2s0", h2stg[:, :], t_phaseA)
                tap("ga_bc", ga_bc[:, 0:D], t_phaseA)
            if debug:
                t_phaseA = t_phaseA + [pr.tok("do")]

        if do_passb:
            with ExitStack() as bctx:
                EB = bctx.enter_context
                wG = EB(nc.sbuf_tensor("wG", [P, 2 * DT * D], BF16))
                wU = EB(nc.sbuf_tensor("wU", [P, 2 * DT * D], BF16))
                wD = EB(nc.sbuf_tensor("wD", [P, DT * D], BF16))
                sg2 = EB(nc.sbuf_tensor("sg2", [P, 512], BF16))
                ub2 = EB(nc.sbuf_tensor("ub2", [P, 512], BF16))
                obuf = EB(nc.sbuf_tensor("obuf", [P, D], F32))

                def pe_srcs(kind, idx):
                    if kind == "sh":
                        return (ap(d_sg_scr, idx * P * DT * D, [[DT * D, P], [1, DT * D]]),
                                ap(d_su_scr, idx * P * DT * D, [[DT * D, P], [1, DT * D]]),
                                ap(d_sd_scr, idx * P * DT * D, [[DT * D, P], [1, DT * D]]),
                                [wprep_toks["sgw"], wprep_toks["suw"], wprep_toks["sdw"]])
                    return (ap(d_gw_scr, idx * P * DT * D, [[DT * D, P], [1, DT * D]]),
                            ap(d_uw_scr, idx * P * DT * D, [[DT * D, P], [1, DT * D]]),
                            ap(d_dw_scr, idx * P * DT * D, [[DT * D, P], [1, DT * D]]),
                            [wprep_toks["gw"], wprep_toks["uw"], wprep_toks["dw"]])

                wslot_free = {0: list(t_phaseA), 1: list(t_phaseA)}
                wd_free = list(t_phaseA)
                out_toks = []

                acc = EB(nc.sbuf_tensor("acc", [P, (NCH // 2) * D], F32))
                h2b = EB(nc.sbuf_tensor("h2b", [P, 2 * DT * D], BF16))
                act = EB(nc.sbuf_tensor("act", [P, DT * D], BF16))
                pes = [("sh", 0), ("sh", 1)] + [("ex", e2) for e2 in range(NE)]
                t_gumul = [None]
                for th in range(2):
                    t_ac = pr.emit("gpsimd", lambda g: g.memset(acc[:, :], 0.0),
                                   waits=t_phaseA + [pr.tok("vs")], inc="gs")
                    acc_guard = [t_ac]
                    thoff = th * DT * D
                    for ht in range(DT):
                        pr.emit("sync", lambda sy, ht=ht, th=th, thoff=thoff: sy.dma_start(
                            ap(h2b, thoff + ht * D, [[2 * DT * D, P], [P, DT], [1, P]]),
                            ap(d_h2s, th * DT * P * D + P * ht + 0, [[D, P], [P * D, DT], [1, P]])),
                            waits=t_phaseA, inc=f"dh2{th}", inc_by=16)
                    h2_load = [pr.tok(f"dh2{th}")]
                    for pi, (kind, idx) in enumerate(pes):
                        slot = pi % 2
                        gsrc, usrc, dsrc, wtoks = pe_srcs(kind, idx)
                        t_lg_ = pr.emit("sync", lambda sy, gsrc=gsrc, slot=slot: sy.dma_start(
                            ap(wG, slot * DT * D, [[2 * DT * D, P], [1, DT * D]]), gsrc),
                            waits=wtoks + wslot_free[slot], inc=f"dwg{slot}", inc_by=16)
                        t_lu_ = pr.emit("sync", lambda sy, usrc=usrc, slot=slot: sy.dma_start(
                            ap(wU, slot * DT * D, [[2 * DT * D, P], [1, DT * D]]), usrc),
                            waits=wtoks + wslot_free[slot], inc=f"dwu{slot}", inc_by=16)
                        t_ld_ = pr.emit("sync", lambda sy, dsrc=dsrc: sy.dma_start(wD[:, :], dsrc),
                                        waits=wtoks + wd_free, inc="dwd", inc_by=16)
                        for fj in range(DT):
                            for tb in range(2):
                                bA = bk[(fj * 2 + tb) % 2]
                                bB = bk[4 + (fj * 2 + tb) % 2]
                                mmsA, mmsB = [], []
                                for ht in range(DT):
                                    rh = h2b[:, thoff + ht * D + tb * 512:thoff + ht * D + (tb + 1) * 512]
                                    mmsA.append((bA.t[:, 0:512],
                                                 wG[:, slot * DT * D + ht * D + fj * P:
                                                    slot * DT * D + ht * D + (fj + 1) * P], rh))
                                    mmsB.append((bB.t[:, 0:512],
                                                 wU[:, slot * DT * D + ht * D + fj * P:
                                                    slot * DT * D + ht * D + (fj + 1) * P], rh))
                                bA.mm_group(mmsA, extra_waits=[t_lg_] + h2_load)
                                bB.mm_group(mmsB, extra_waits=[t_lu_] + h2_load)
                                t_sg = bA.read("scalar", lambda s, bA=bA: s.activation(
                                    sg2[:, :], bA.t[:, 0:512], AF.Silu),
                                    extra_waits=[t_gumul[0]])
                                t_ub = bB.read("scalar", lambda s, bB=bB: s.activation(
                                    ub2[:, :], bB.t[:, 0:512], AF.Identity),
                                    extra_waits=[t_gumul[0]])
                                t_gumul[0] = pr.emit("vector", lambda v, fj=fj, tb=tb: v.tensor_mul(
                                    act[:, fj * D + tb * 512:fj * D + (tb + 1) * 512],
                                    sg2[:, :], ub2[:, :]),
                                    waits=[t_sg, t_ub], inc="vs")
                        t_act = pr.tok("vs")
                        for tt in range(DT):
                            c16 = th * 8 + tt
                            for hh in range(2):
                                bC = bk[2 + (tt * 2 + hh) % 2]
                                mms = []
                                for fj in range(DT):
                                    mms.append((bC.t[:, 0:512],
                                                ap(act, fj * D + tt * P, [[DT * D, P], [1, P]]),
                                                wD[:, fj * D + hh * 512:fj * D + (hh + 1) * 512]))
                                bC.mm_group(mms, extra_waits=[t_act, t_ld_])
                                if kind == "sh":
                                    csl = onef_col[:, 0:1]
                                else:
                                    csl = comb_all[:, c16 * NE + idx:c16 * NE + idx + 1]
                                bC.read("vector", lambda v, bC=bC, tt=tt, hh=hh, csl=csl: v.scalar_tensor_tensor(
                                    acc[:, tt * D + hh * 512:tt * D + (hh + 1) * 512],
                                    bC.t[:, 0:512], csl,
                                    acc[:, tt * D + hh * 512:tt * D + (hh + 1) * 512],
                                    op0=ALU.mult, op1=ALU.add),
                                    extra_waits=acc_guard)
                        wslot_free[slot] = [pr.tok("ts")]
                        wd_free = [pr.tok("ts")]
                    t_accdone = pr.tok("vs")
                    for cc in range(8):
                        c16 = th * 8 + cc
                        slot = cc % 2
                        t_xr = pr.emit("sync", lambda sy, c16=c16, slot=slot: sy.dma_start(
                            ap(xb, slot * D, [[2 * D, P], [1, D]]),
                            ap(d_x2s, c16 * P * D, [[D, P], [1, D]])),
                            waits=[t_accdone] + ([pr.tok("vs")] if cc >= 2 else []),
                            inc=f"dx{slot}", inc_by=16)
                        t_ob = pr.emit("vector", lambda v, cc=cc: v.tensor_mul(
                            obuf[:, :], acc[:, cc * D:(cc + 1) * D], gb_bc[:, :]),
                            waits=[t_accdone, pr.tok("do")], inc="vs")
                        t_o2 = pr.emit("vector", lambda v, slot=slot: v.tensor_add(
                            obuf[:, :], obuf[:, :], xb[:, slot * D:(slot + 1) * D]),
                            waits=[t_xr, t_ob], inc="vs")
                        out_toks.append(pr.emit("sync", lambda sy, c16=c16: sy.dma_start(
                            ap(d_out, c16 * P * D, [[D, P], [1, D]]), obuf[:, :]),
                            waits=[t_o2], inc="do", inc_by=16))

                pr.q["sync"].append(([out_toks[-1]], lambda sy: None, None))
                pr.lastw["sync"][out_toks[-1][0]] = out_toks[-1][1]

        else:
            pr.q["sync"].append(([t_phaseA[0]], lambda sy: None, None))

        block = octx.enter_context(nc.Block())

        @block.tensor
        def _(t):
            pr.replay("tensor", t)

        @block.vector
        def _(v):
            pr.replay("vector", v)

        @block.scalar
        def _(s):
            pr.replay("scalar", s)

        @block.gpsimd
        def _(g):
            pr.replay("gpsimd", g)

        @block.sync
        def _(sy):
            pr.replay("sync", sy)

    return nc, taps


def _mask(fn):
    r = np.arange(P)[:, None]
    c = np.arange(P)[None, :]
    return fn(r, c).astype(np.float32).astype(ml_dtypes.bfloat16)


def make_inputs(L0_attn_in_seq, L0_attn_theta_K, L0_attn_theta_V, L0_attn_theta_Q,
                L0_attn_W0, L0_attn_b0, L0_ffn_router_w, L0_ffn_gate_w, L0_ffn_up_w,
                L0_ffn_down_w, L0_ffn_shared_gate_w, L0_ffn_shared_up_w,
                L0_ffn_shared_down_w, L0_c, L0_wmod):
    x = np.ascontiguousarray(np.asarray(L0_attn_in_seq, dtype=np.float32))
    f32 = lambda a: np.ascontiguousarray(np.asarray(a, dtype=np.float32))
    base = {
        "thK": f32(np.asarray(L0_attn_theta_K).T),
        "thV": f32(np.asarray(L0_attn_theta_V).T),
        "thQ": f32(np.asarray(L0_attn_theta_Q).T),
        "w0T": f32(np.asarray(L0_attn_W0).T),
        "b0r": f32(np.asarray(L0_attn_b0).reshape(1, D)),
        "rw": f32(L0_ffn_router_w),
        "gw": f32(L0_ffn_gate_w), "uw": f32(L0_ffn_up_w), "dw": f32(L0_ffn_down_w),
        "sgw": f32(L0_ffn_shared_gate_w), "suw": f32(L0_ffn_shared_up_w),
        "sdw": f32(L0_ffn_shared_down_w),
        "wmod": f32(L0_wmod),
        "identb": np.eye(P, dtype=np.float32).astype(ml_dtypes.bfloat16),
        "mune": _mask(lambda r, c: np.where(r < c, -1.0, 0.0)),
        "mlne": _mask(lambda r, c: np.where(r > c, -1.0, 0.0)),
        "muinc": _mask(lambda r, c: np.where(r <= c, 1.0, 0.0)),
    }
    cmat = f32(L0_c)
    in_maps = []
    for k in range(8):
        b, h = k >> 1, k & 1
        gm = np.zeros((P, 2), dtype=np.float32)
        gm[:, 0] = -2.0 * LR / D * h
        gm[:, 1] = -2.0 * LR / D
        m = dict(base)
        m["xm"] = np.ascontiguousarray(x[b, h * TOWN:(h + 1) * TOWN])
        m["xp"] = np.ascontiguousarray(x[b, 0:TOWN])
        m["gmask"] = gm
        m["cvec"] = np.ascontiguousarray(cmat[b].reshape(D, 1))
        in_maps.append(m)
    return in_maps


def kernel(**inputs):
    in_maps = make_inputs(**inputs)
    nc, _ = build_program(debug=False)
    res = run_bass_kernel_spmd(nc, in_maps, core_ids=list(range(8)))
    out = np.empty((4, 4096, D), dtype=np.float32)
    for k in range(8):
        b, h = k >> 1, k & 1
        out[b, h * TOWN:(h + 1) * TOWN] = res.results[k]["out"]
    return out
